# version 10
# baseline (speedup 1.0000x reference)
import bisect
import numpy as np
import ml_dtypes
import concourse.bass as bass
import concourse.mybir as mybir
from concourse.bass_utils import run_bass_kernel_spmd
from contextlib import ExitStack

F32 = mybir.dt.float32
BF16 = mybir.dt.bfloat16
I32 = mybir.dt.int32
I16 = mybir.dt.int16
AF = mybir.ActivationFunctionType
ALU = mybir.AluOpType
AXX = mybir.AxisListType.X

B, T, D, H, DK = 4, 2048, 2048, 16, 128
TH = T // 2
NC1 = 16384
SCALE = float(DK ** -0.5)
P126 = float(2.0 ** 126)
GELT_A = 0.7978845608028654
GELT_B = 0.044715 * GELT_A


def perm_pi(n):
    out = np.empty(n, np.int64)
    idx = np.arange(n).reshape(n // 256, 256)
    for b_ in range(n // 256):
        out[b_ * 256:b_ * 256 + 128] = idx[b_][0::2]
        out[b_ * 256 + 128:b_ * 256 + 256] = idx[b_][1::2]
    return out


PI_D = perm_pi(D)
PI_N = perm_pi(NC1)
INV_PI_D = np.argsort(PI_D)



class SV:
    __slots__ = ("ap", "key")

    def __init__(self, ap, key):
        self.ap = ap
        self.key = key

    def __getitem__(self, sl):
        return SV(self.ap[sl], self.key)


def _ap(x):
    return x.ap if isinstance(x, SV) else x


def _k(x):
    if isinstance(x, SV):
        return x.key
    t = getattr(x, "tensor", None)
    if t is not None and hasattr(t, "name"):
        return t.name
    if hasattr(x, "name"):
        return str(x.name)
    return str(id(x))


ENGS = ("pe", "dve", "act", "pool", "sp")


class Rec:
    __slots__ = ("eng", "fn", "reads", "writes", "seq", "waits", "inc", "iss",
                 "tag", "sync_out")


class Prog:

    NDMA = 10

    def __init__(self, nc):
        self.nc = nc
        self.recs = []
        self.dsubs = tuple("dg%d" % i for i in range(self.NDMA)) + \
                     tuple("ds%d" % i for i in range(self.NDMA))
        self.counts = {e: 0 for e in ENGS + self.dsubs}
        self.rr = {"dg": 0, "ds": 0}
        self.last_w = {}
        self.readers = {}

    def _record(self, eng, fn, reads, writes, iss=None, tag="", sync_out=False):
        r = Rec()
        r.eng, r.fn, r.tag = eng, fn, tag
        r.sync_out = sync_out
        r.iss = iss or eng
        r.reads = [_k(a) for a in reads]
        r.writes = [_k(a) for a in writes]
        self.counts[eng] += 1
        r.seq = self.counts[eng]
        deps = {}

        def need(e2, s2, force=False):
            if e2 == eng and e2 in ENGS and not force:
                return
            if deps.get(e2, 0) < s2:
                deps[e2] = s2

        for b_ in r.reads:
            lw = self.last_w.get(b_)
            if lw:
                need(lw[0], lw[1], force=(len(lw) > 2 and lw[2]))
        for b_ in r.writes:
            lw = self.last_w.get(b_)
            if lw:
                need(lw[0], lw[1])
            for e2, s2 in self.readers.get(b_, {}).items():
                need(e2, s2)
        r.waits = sorted(deps.items())
        for b_ in r.writes:
            self.last_w[b_] = (eng, r.seq, sync_out)
            self.readers[b_] = {}
        for b_ in r.reads:
            d = self.readers.setdefault(b_, {})
            if d.get(eng, 0) < r.seq:
                d[eng] = r.seq
        self.recs.append(r)
        return r

    def barrier(self):
        cur = dict(self.counts)
        for e in ENGS:
            def fn(h):
                return h.nop()
            r = Rec()
            r.eng, r.iss, r.fn, r.tag = e, e, fn, "barrier"
            r.reads, r.writes = [], []
            self.counts[e] += 1
            r.seq = self.counts[e]
            r.waits = [(e2, s) for e2, s in cur.items() if e2 != e and s > 0]
            self.recs.append(r)
        self.last_w = {}
        self.readers = {}

    def _dma_sub(self, queue):
        sub = "%s%d" % (queue, self.rr[queue] % self.NDMA)
        self.rr[queue] += 1
        return sub

    def dma(self, queue, out, in_, accum=False, tag=""):
        iss = "pool" if queue == "dg" else "sp"
        sub = self._dma_sub(queue)
        o, i = _ap(out), _ap(in_)
        kw = {}
        if accum:
            kw["accum_op"] = ALU.add

        def fn(h):
            return h.dma_start(out=o, in_=i, **kw)
        rd = [in_] + ([out] if accum else [])
        r = self._record(sub, fn, rd, [out], iss=iss, tag=tag)
        if r.seq > 1:
            d = dict(r.waits)
            if d.get(sub, 0) < r.seq - 1:
                d[sub] = r.seq - 1
            r.waits = sorted(d.items())
        return r

    def mm(self, out, lhsT, rhs, start=True, stop=True, tag=""):
        o, l_, r_ = _ap(out), _ap(lhsT), _ap(rhs)

        def fn(h):
            return h.matmul(o, l_, r_, start=start, stop=stop)
        return self._record("pe", fn, [lhsT, rhs], [out], tag=tag)

    def tr(self, out, in_, ident, tag=""):
        o, i, d = _ap(out), _ap(in_), _ap(ident)

        def fn(h):
            return h.transpose(o, i, d)
        return self._record("pe", fn, [in_, ident], [out], tag=tag)

    def tt(self, eng, out, a, b_, op, tag=""):
        o, x, y = _ap(out), _ap(a), _ap(b_)

        def fn(h):
            return h.tensor_tensor(o, x, y, op)
        return self._record(eng, fn, [a, b_], [out], tag=tag)

    def ts(self, eng, out, a, s1, s2=None, op0=ALU.mult, op1=ALU.bypass, tag=""):
        o, x = _ap(out), _ap(a)
        v1 = _ap(s1) if isinstance(s1, SV) else s1
        v2 = _ap(s2) if isinstance(s2, SV) else s2
        rd = [a] + [s for s in (s1, s2) if isinstance(s, SV) or hasattr(s, "tensor")]

        def fn(h):
            if s2 is None:
                return h.tensor_single_scalar(o, x, v1, op0)
            return h.tensor_scalar(o, x, v1, v2, op0, op1)
        return self._record(eng, fn, rd, [out], tag=tag)

    def stt(self, eng, out, in0, scalar, in1, op0, op1, tag=""):
        o, x, y = _ap(out), _ap(in0), _ap(in1)
        sc = _ap(scalar) if isinstance(scalar, SV) else scalar
        rd = [in0, in1] + ([scalar] if isinstance(scalar, SV) or hasattr(scalar, "tensor") else [])

        def fn(h):
            return h.scalar_tensor_tensor(o, x, sc, y, op0, op1)
        return self._record(eng, fn, rd, [out], tag=tag)

    def act(self, out, in_, func, bias=0.0, scale=1.0, tag=""):
        o, i = _ap(out), _ap(in_)
        bb = _ap(bias) if isinstance(bias, SV) else bias
        ss = _ap(scale) if isinstance(scale, SV) else scale
        rd = [in_] + [z for z in (bias, scale) if isinstance(z, SV)]

        def fn(h):
            return h.activation(o, i, func, bias=bb, scale=ss)
        return self._record("act", fn, rd, [out], tag=tag)

    def copy(self, eng, out, in_, tag=""):
        o, i = _ap(out), _ap(in_)
        if eng == "act":
            def fn(h):
                return h.copy(o, i)
        else:
            def fn(h):
                return h.tensor_copy(o, i)
        return self._record(eng, fn, [in_], [out], tag=tag)

    def red(self, out, in_, op, tag=""):
        o, i = _ap(out), _ap(in_)

        def fn(h):
            if op == "max":
                return h.reduce_max(o, i, axis=AXX)
            return h.reduce_sum(o, i, axis=AXX)
        return self._record("dve", fn, [in_], [out], tag=tag, sync_out=True)

    def recip(self, out, in_, tag=""):
        o, i = _ap(out), _ap(in_)

        def fn(h):
            return h.reciprocal(o, i)
        return self._record("dve", fn, [in_], [out], tag=tag, sync_out=True)

    def cumsum(self, out, in_, tag=""):
        o, i = _ap(out), _ap(in_)

        def fn(h):
            return h.tensor_tensor_scan(o, i, i, 0.0, ALU.add, ALU.bypass)
        return self._record("dve", fn, [in_], [out], tag=tag, sync_out=True)

    def memset(self, eng, ap, v, tag=""):
        o = _ap(ap)

        def fn(h):
            return h.memset(o, v)
        return self._record(eng, fn, [], [ap], tag=tag)

    def dma_gather(self, out, in_, idx, tag=""):
        sub = self._dma_sub("dg")
        o, i, ix = _ap(out), _ap(in_), _ap(idx)

        def fn(h):
            return h.indirect_dma_start(o, None, i, bass.IndirectOffsetOnAxis(ix, 1))
        r = self._record(sub, fn, [in_, idx], [out], iss="pool", tag=tag)
        if r.seq > 1:
            d = dict(r.waits)
            if d.get(sub, 0) < r.seq - 1:
                d[sub] = r.seq - 1
            r.waits = sorted(d.items())
        return r

    def materialize(self, block, sems):
        byiss = {}
        for r in self.recs:
            byiss.setdefault(r.iss, []).append(r)
        for e, lst in byiss.items():
            water = {}
            for r in lst:
                kept = []
                for (e2, s2) in r.waits:
                    if water.get(e2, -1) >= s2:
                        continue
                    kept.append((e2, s2))
                    water[e2] = s2
                r.waits = kept
        targets = {}
        for r in self.recs:
            for (e2, s2) in r.waits:
                targets.setdefault(e2, set()).add(s2)
        incd = {e: sorted(targets.get(e, ())) for e in ENGS}
        for r in self.recs:
            if r.eng in ENGS:
                r.inc = r.seq in targets.get(r.eng, ())
            else:
                r.inc = True

        def wval(e, s):
            if e not in ENGS:
                return 16 * s
            arr = incd[e]
            i = bisect.bisect_left(arr, s)
            assert i < len(arr), (e, s)
            return i + 1

        streams = {e: [] for e in ENGS}
        for r in self.recs:
            streams[r.iss].append(r)

        handles = {"pe": block.tensor, "dve": block.vector, "act": block.scalar,
                   "pool": block.gpsimd, "sp": block.sync}

        final_d = {s: 16 * self.counts[s] for s in self.dsubs}

        for ename in ENGS:
            deco = handles[ename]

            def body(h, _ename=ename):
                for r in streams[_ename]:
                    for (e2, s2) in r.waits:
                        h.wait_ge(sems[e2], wval(e2, s2))
                    try:
                        ins = r.fn(h)
                    except Exception as ex:
                        raise RuntimeError(
                            f"emit failed eng={r.eng} seq={r.seq} tag={r.tag!r} "
                            f"reads={r.reads} writes={r.writes}: {ex}") from ex
                    if ins is None:
                        continue
                    if r.eng not in ENGS:
                        ins.then_inc(sems[r.eng], 16)
                    elif r.inc:
                        ins.then_inc(sems[r.eng], 1)
                for s, v in final_d.items():
                    if v == 0:
                        continue
                    if (_ename == "sp" and s.startswith("ds")) or \
                       (_ename == "pool" and s.startswith("dg")):
                        h.wait_ge(sems[s], v)
            deco(body)



def build(dbg=(), upto=99, ln_stage=3):
    nc = bass.Bass("TRN2", target_bir_lowering=False)
    dt = nc.dram_tensor

    def inp(name, shape, dtype=F32):
        return dt(name, list(shape), dtype, kind="ExternalInput").ap()

    x0f = inp("x0f", [D, T])
    wq = inp("l0_wq", [D, D]); wk = inp("l0_wk", [D, D])
    wv = inp("l0_wv", [D, D]); ww = inp("l0_ww", [D, D])
    cq = inp("l0_cq", [D, 4]); ck = inp("l0_ck", [D, 4]); cv = inp("l0_cv", [D, 4])
    wf1 = inp("l0_wf1", [D, 128]); wf2 = inp("l0_wf2", [128, D])
    wb = inp("l0_wb", [D, 16]); wg1 = inp("l0_wg1", [D, 128]); wg2 = inp("l0_wg2", [128, D])
    bg2c = inp("l0_bg2", [128, 16]); nw0 = inp("l0_normw", [128, 1])
    wo0 = inp("l0_wo", [D, D])
    w0t = inp("l0_w0t", [D, 8192]); b0c = inp("l0_b0", [128, 64])
    w1t = inp("l0_w1t", [8192, D]); b1c = inp("l0_b1", [128, 16])
    wqt1 = inp("l1_wqt", [D, D]); wkt1 = inp("l1_wkt", [D, D]); wvt1 = inp("l1_wvt", [D, D])
    fq1t = inp("l1_fq1t", [128, 128]); fq2t = inp("l1_fq2t", [128, 128])
    fk1t = inp("l1_fk1t", [128, 128]); fk2t = inp("l1_fk2t", [128, 128])
    fq1b = inp("l1_fq1b", [1, 128]); fq2b = inp("l1_fq2b", [1, 128])
    fk1b = inp("l1_fk1b", [1, 128]); fk2b = inp("l1_fk2b", [1, 128])
    nw1 = inp("l1_normw", [128, 1])
    wot1 = inp("l1_wot", [D, D])
    tg = inp("l1_tg", [1024, NC1], I32); tu = inp("l1_tu", [1024, NC1], I32)
    gsc = inp("l1_gsc", [64, NC1]); usc = inp("l1_usc", [64, NC1])
    td = inp("l1_td", [8192, D], I32); dscx = inp("l1_dscx", [8192, D])
    jidx = inp("jidx", [128, 1], I32)
    id_bf = inp("ident_bf", [128, 128], BF16)
    id_f = inp("ident_f32", [128, 128])
    mask128 = inp("mask128", [128, 128])
    mask64 = inp("mask64", [64, 64])
    ones1 = inp("ones1", [1, 128])
    onescol = inp("onescol", [128, 1])
    oh127 = inp("oh127", [128, 1])
    oh16 = inp("oh16", [16, 16])
    pexp8 = inp("pexp8", [8, 128])

    out = dt("out", [D, TH], F32, kind="ExternalOutput").ap()
    dbg_out = {}
    for name, shape, dty in (
        ("d_xn0", [D, T], BF16), ("d_o0", [D, T], BF16), ("d_s0", [D, T], F32),
        ("d_e0", [D, T], F32), ("d_x1", [D, T], F32), ("d_xn1", [D, T], BF16),
        ("d_o1", [D, T], BF16), ("d_s1", [D, T], F32), ("d_inter", [NC1, TH], BF16),
        ("d_down", [D, TH], F32), ("d_h0", [128, 8 * 2048], F32),
        ("d_c0", [128, 16 * 128], F32),
    ):
        if name in dbg:
            dbg_out[name] = dt(name, shape, dty, kind="ExternalOutput").ap()

    o_hbm = dt("o_hbm", [D, T], BF16, kind="Internal").ap()
    s0_hbm = dt("s0_hbm", [D, T], F32, kind="Internal").ap()
    e_hbm = dt("e_hbm", [D, T], F32, kind="Internal").ap()
    x1_hbm = dt("x1_hbm", [D, T], F32, kind="Internal").ap()
    xn1_hbm = dt("xn1_hbm", [D, T], BF16, kind="Internal").ap()
    s1_hbm = dt("s1_hbm", [D, T], F32, kind="Internal").ap()
    inter_hbm = dt("inter_hbm", [NC1, TH], BF16, kind="Internal").ap()
    down_hbm = dt("down_hbm", [D, TH], F32, kind="Internal").ap()

    with ExitStack() as st_:
        e_ = st_.enter_context
        ar_xn = e_(nc.sbuf_tensor("ar_xn", [128, 32768], BF16))
        ar_b = e_(nc.sbuf_tensor("ar_b", [128, 32768], BF16))
        ar_c = e_(nc.sbuf_tensor("ar_c", [128, 8192], F32))
        ar_d = e_(nc.sbuf_tensor("ar_d", [128, 14336], BF16))
        sm32 = e_(nc.sbuf_tensor("sm32", [128, 2048], F32))
        smb = e_(nc.sbuf_tensor("smb", [128, 2688], BF16))
        cst = e_(nc.sbuf_tensor("cst", [128, 720], F32))
        cstb = e_(nc.sbuf_tensor("cstb", [128, 512], BF16))
        pA = e_(nc.psum_tensor("pA", [128, 512], F32))
        pB = e_(nc.psum_tensor("pB", [128, 512], F32))
        pC = e_(nc.psum_tensor("pC", [128, 512], F32))
        pD = e_(nc.psum_tensor("pD", [128, 512], F32))
        pE = e_(nc.psum_tensor("pE", [128, 512], F32))
        pF = e_(nc.psum_tensor("pF", [128, 512], F32))
        pG = e_(nc.psum_tensor("pG", [128, 512], F32))
        pH = e_(nc.psum_tensor("pH", [128, 512], F32))
        sems = {}
        for s in ("pe", "dve", "act", "pool", "sp"):
            sems[s] = e_(nc.semaphore("sem_" + s))
        for i_ in range(Prog.NDMA):
            sems["dg%d" % i_] = e_(nc.semaphore("sem_dg%d" % i_))
            sems["ds%d" % i_] = e_(nc.semaphore("sem_ds%d" % i_))
        block = e_(nc.Block())

        p = Prog(nc)

        def V(tens, c0, c1, key):
            return SV(tens.ap()[:, c0:c1], key)

        d32 = ar_d.ap().bitcast(F32)
        smbf = smb.ap().bitcast(F32)
        smbi = smb.ap().bitcast(I16)

        def xnv(dc, a, b_):
            return SV(ar_xn.ap()[:, dc * 2048 + a: dc * 2048 + b_], "xn_%d" % dc)

        c_idbf = SV(cstb.ap()[:, 0:128], "c_idbf")
        c_m128b = SV(cstb.ap()[:, 128:256], "c_m128b")
        c_idf = SV(cst.ap()[:, 0:128], "c_idf")
        c_m128 = SV(cst.ap()[:, 128:256], "c_m128")
        c_m64 = SV(cst.ap()[0:64, 256:320], "c_m64")
        c_ones1 = SV(cst.ap()[0:1, 320:448], "c_ones1")
        c_onescol = SV(cst.ap()[:, 448:449], "c_onescol")
        c_oh127 = SV(cst.ap()[:, 449:450], "c_oh127")
        p.dma("ds", c_idf, id_f)
        p.dma("ds", c_m128, mask128)
        p.dma("ds", c_m64, mask64)
        p.dma("ds", c_ones1, ones1)
        p.dma("ds", c_onescol, onescol)
        p.dma("ds", c_oh127, oh127)
        p.dma("dg", c_idbf, id_bf)
        p.dma("dg", c_m128b, mask128)
        cqs = SV(cst.ap()[:, 450:454], "c_cq")
        cks = SV(cst.ap()[:, 454:458], "c_ck")
        cvs = SV(cst.ap()[:, 458:462], "c_cv")
        c_bg2 = SV(cst.ap()[:, 462:478], "c_bg2")
        c_b0 = SV(cst.ap()[:, 478:542], "c_b0")
        c_b1 = SV(cst.ap()[:, 542:558], "c_b1")
        c_nw0 = SV(cst.ap()[:, 558:559], "c_nw0")
        c_nw1 = SV(cst.ap()[:, 559:560], "c_nw1")
        c_pexp8 = SV(cst.ap()[0:8, 564:692], "c_pexp8")
        c_jidx = SV(cst.ap()[:, 692:693].bitcast(I32), "c_jidx")
        c_eps5 = SV(cst.ap()[:, 712:713], "c_eps5")
        c_eps6 = SV(cst.ap()[:, 713:714], "c_eps6")
        c_one = SV(cst.ap()[:, 714:715], "c_one")
        idxw = SV(cst.ap()[:, 715:716].bitcast(I32), "idxw")
        p.dma("ds", c_bg2, bg2c)
        p.dma("ds", c_b0, b0c)
        p.dma("ds", c_b1, b1c)
        p.dma("ds", c_nw0, nw0)
        p.dma("ds", c_nw1, nw1)
        p.dma("ds", c_pexp8, pexp8)
        p.dma("ds", c_jidx, jidx)
        p.memset("dve", c_eps5, 1e-5)
        p.memset("dve", c_eps6, 1e-6)
        p.memset("dve", c_one, 1.0)
        c_fq1t = SV(cstb.ap()[:, 256:384], "c_fq1t")
        c_fq2t = SV(cstb.ap()[:, 384:512], "c_fq2t")
        cb2 = e_(nc.sbuf_tensor("cstb2", [128, 512], BF16))
        c_fk1t = SV(cb2.ap()[:, 0:128], "c_fk1t")
        c_fk2t = SV(cb2.ap()[:, 128:256], "c_fk2t")
        sv_wf2 = SV(ar_b.ap()[:, 28672:30720], "c_wf2")
        sv_wg2 = SV(ar_b.ap()[:, 30720:32768], "c_wg2")
        c_oh16 = SV(cst.ap()[0:16, 696:712], "c_oh16")
        p.dma("dg", c_fq1t, fq1t)
        p.dma("dg", c_fq2t, fq2t)
        p.dma("dg", c_fk1t, fk1t)
        p.dma("dg", c_fk2t, fk2t)
        p.dma("ds", c_oh16, oh16)

        if upto <= -1:
            p.materialize(block, sems)
            return nc, sorted(dbg_out)

        TT = (0, 512, 1024, 1536)

        def ln_pass(src_hbm, xn_writer, tag):
            xc = SV(sm32.ap()[:, 0:2048], "ln_xc")
            sq = SV(smbf[:, 0:512], "ln_sq")
            mrow = [SV(pX.ap()[0:1, 0:512], "p%s" % n)
                    for pX, n in ((pA, "A"), (pB, "B"), (pC, "C"), (pD, "D"))]
            vrow = [SV(pE.ap()[0:1, 0:512], "pE"), SV(pF.ap()[0:1, 0:512], "pF"),
                    SV(pG.ap()[0:1, 0:512], "pG"),
                    SV(pH.ap()[0:1, 0:512], "pH")]
            for dc in range(16):
                p.dma("ds", xc, SV(src_hbm[dc * 128:dc * 128 + 128, :], "hbm_src"), tag=tag)
                for ti in range(4):
                    xt = xc[:, TT[ti]:TT[ti] + 512]
                    p.mm(mrow[ti], c_onescol, xt, start=(dc == 0), stop=(dc == 15))
                    p.act(sq, xt, AF.Square)
                    p.mm(vrow[ti], c_onescol, sq, start=(dc == 0), stop=(dc == 15))
            if ln_stage <= 1:
                return
            rowm = SV(ar_c.ap()[0:1, 5120:5632], "rowm")
            rowv = SV(ar_c.ap()[0:1, 4096:4608], "rowv")
            rowr = SV(ar_c.ap()[0:1, 4608:5120], "rowr")
            mB = SV(ar_c.ap()[:, 0:2048], "ln_mB")
            rB = SV(ar_c.ap()[:, 2048:4096], "ln_rB")
            for ti in range(4):
                p.ts("dve", rowm, mrow[ti], 1.0 / D)
                p.ts("dve", rowv, vrow[ti], 1.0 / D)
                p.act(rowr, rowm, AF.Square)
                p.tt("dve", rowv, rowv, rowr, ALU.subtract)
                p.act(rowr, rowv, AF.Sqrt, bias=c_eps5[0:1, :])
                p.recip(rowr, rowr)
                p.mm(SV(pA.ap()[:, 0:512], "pA"), c_ones1, rowm)
                p.mm(SV(pE.ap()[:, 0:512], "pE"), c_ones1, rowr)
                p.copy("dve", mB[:, TT[ti]:TT[ti] + 512], SV(pA.ap()[:, 0:512], "pA"))
                p.copy("act", rB[:, TT[ti]:TT[ti] + 512], SV(pE.ap()[:, 0:512], "pE"))
            self_barrier = p.barrier()
            mB = SV(ar_c.ap()[:, 0:2048], "ln_mB")
            rB = SV(ar_c.ap()[:, 2048:4096], "ln_rB")
            if ln_stage <= 2:
                return
            tmpf = SV(sm32.ap()[:, 0:2048], "ln_xc")
            for dc in range(16):
                p.dma("ds", tmpf, SV(src_hbm[dc * 128:dc * 128 + 128, :], "hbm_src"), tag=tag)
                sub = SV(sm32.ap()[:, 0:2048], "ln_xc")
                p.tt("dve", sub, tmpf, mB, ALU.subtract)
                xn_writer(dc, sub, rB)

        def hbm_rows(hbm, r0, r1, key=None):
            return SV(hbm[r0:r1, :], key or _k(hbm))

        def load_wtiles(queue, dst_sv, hbm_slice, key_src, ccols=128):
            src = hbm_slice.rearrange("(c p) k -> p c k", p=128)
            dstv = SV(dst_sv.ap.rearrange("p (c k) -> p c k", k=ccols), dst_sv.key)
            p.dma(queue, dstv, SV(src, key_src))

        def w_xn0(dc, sub, rB):
            p.tt("dve", xnv(dc, 0, 2048), sub, rB, ALU.mult)
            if "d_xn0" in dbg_out:
                p.dma("ds", SV(dbg_out["d_xn0"][dc * 128:dc * 128 + 128, :], "d_xn0"),
                      xnv(dc, 0, 2048))
        ln_pass(x0f, w_xn0, "ln0")
        p.barrier()

        if upto <= 0:
            p.materialize(block, sems)
            return nc, sorted(dbg_out)

        p.dma("dg", sv_wf2, SV(wf2, "l0_wf2"))
        p.dma("dg", sv_wg2, SV(wg2, "l0_wg2"))
        wsl1 = SV(ar_b.ap()[:, 16384:18432], "wsl1")
        z1 = SV(ar_b.ap()[:, 24576:26624], "z1")
        g1t = SV(ar_b.ap()[:, 26624:28672], "g1t")
        beta_sb = SV(d32[0:16, 5120:7168], "beta_sb")
        for mat, matk, dst in ((wf1, "l0_wf1", z1), (wg1, "l0_wg1", g1t)):
            load_wtiles("dg", wsl1, mat, matk)
            for ti in range(4):
                pa = (SV(pA.ap()[:, 0:512], "pA") if ti % 2 == 0
                      else SV(pB.ap()[:, 0:512], "pB"))
                for dc in range(16):
                    p.mm(pa, wsl1[:, dc * 128:dc * 128 + 128],
                         xnv(dc, TT[ti], TT[ti] + 512), start=(dc == 0), stop=(dc == 15))
                p.copy("act", dst[:, TT[ti]:TT[ti] + 512], pa)
        wbsl = SV(ar_b.ap()[:, 16384:16640], "wsl1")
        load_wtiles("dg", wbsl, wb, "l0_wb", ccols=16)
        for ti in range(4):
            pa = (SV(pA.ap()[0:16, 0:512], "pA") if ti % 2 == 0
                  else SV(pB.ap()[0:16, 0:512], "pB"))
            for dc in range(16):
                p.mm(pa, wbsl[:, dc * 16:dc * 16 + 16],
                     xnv(dc, TT[ti], TT[ti] + 512), start=(dc == 0), stop=(dc == 15))
            p.act(beta_sb[:, TT[ti]:TT[ti] + 512], pa, AF.Sigmoid)
        p.barrier()

        if upto <= 1:
            p.materialize(block, sems)
            return nc, sorted(dbg_out)
        beta_sb = SV(d32[0:16, 5120:7168], "beta_sb")

        hq = SV(ar_b.ap()[:, 0:2048], "hq")
        hk = SV(ar_b.ap()[:, 2048:4096], "hk")
        hv = SV(ar_b.ap()[:, 4096:6144], "hv")
        hw = SV(ar_b.ap()[:, 6144:8192], "hw")
        hU = SV(ar_b.ap()[:, 8192:10240], "hU")
        hkt = SV(ar_b.ap()[:, 10240:12288], "hkt")
        hvt = SV(ar_b.ap()[:, 12288:14336], "hvt")
        hUt = SV(ar_b.ap()[:, 14336:16384], "hUt")
        hwq = SV(ar_b.ap()[:, 16384:18432], "hwq")
        hwk = SV(ar_b.ap()[:, 18432:20480], "hwk")
        hwv = SV(ar_b.ap()[:, 20480:22528], "hwv")
        hww = SV(ar_b.ap()[:, 22528:24576], "hww")
        cG = SV(ar_c.ap()[:, 0:2048], "cG")
        cGc = SV(ar_c.ap()[:, 2048:4096], "cGc")
        cgbw = SV(ar_c.ap()[:, 4096:6144], "cgbw")
        cgate = SV(ar_c.ap()[:, 6144:8192], "cgate")
        f1 = SV(d32[:, 0:2048], "f1")
        f6 = SV(d32[:, 2048:4096], "f6")
        o_h = SV(ar_d.ap()[:, 8192:10240], "o_h")
        st_et = SV(sm32.ap()[:, 0:128], "st_et")
        st_sk = SV(sm32.ap()[:, 128:256], "st_sk")
        st_sv = SV(sm32.ap()[:, 256:384], "st_sv")
        st_tmp = SV(sm32.ap()[:, 384:512], "st_tmp")
        st_tmp2 = SV(sm32.ap()[:, 512:640], "st_tmp2")
        dcol = SV(sm32.ap()[:, 640:656], "dcol")
        egm = SV(sm32.ap()[:, 656:672], "egm")
        colmx = SV(sm32.ap()[:, 672:673], "colmx")
        colmx2 = SV(sm32.ap()[:, 673:674], "colmx2")
        colsm = SV(sm32.ap()[:, 674:675], "colsm")
        colrc = SV(sm32.ap()[:, 675:676], "colrc")
        st_sk0s = SV(smb.ap()[:, 0:128], "st_sk0s")
        st_sv0s = SV(smb.ap()[:, 128:256], "st_sv0s")
        st_w2t = SV(smb.ap()[:, 256:384], "st_w2t")
        st_atm = SV(smb.ap()[:, 384:512], "st_atm")
        st_at2 = SV(smb.ap()[:, 512:640], "st_at2")
        stg = SV(smb.ap()[:, 640:1152], "stg")
        row_a = SV(smbf[0:1, 576:1088], "row_a")
        row_a128 = SV(smbf[0:1, 1088:1216], "row_a128")
        row_b128 = SV(smbf[0:1, 1216:1344], "row_b128")

        def proj4(wts, dsts):
            bks = [("pA", pA), ("pB", pB), ("pE", pE), ("pF", pF)]
            for ti in range(4):
                for mi in range(4):
                    kname, bank = bks[mi]
                    pa = SV(bank.ap()[:, 0:512], kname)
                    for dc in range(16):
                        p.mm(pa, wts[mi][:, dc * 128:dc * 128 + 128],
                             xnv(dc, TT[ti], TT[ti] + 512), start=(dc == 0), stop=(dc == 15))
                for mi in range(4):
                    kname, bank = bks[mi]
                    pa = SV(bank.ap()[:, 0:512], kname)
                    p.copy("act" if mi % 2 == 0 else "dve",
                           dsts[mi][:, TT[ti]:TT[ti] + 512], pa)

        def proj_conv(wt, cwv, out_bf, out_f32):
            for ti in range(4):
                pa = (SV(pA.ap()[:, 0:512], "pA") if ti % 2 == 0
                      else SV(pB.ap()[:, 0:512], "pB"))
                for dc in range(16):
                    p.mm(pa, wt[:, dc * 128:dc * 128 + 128],
                         xnv(dc, TT[ti], TT[ti] + 512), start=(dc == 0), stop=(dc == 15))
                p.copy("act" if ti % 2 == 0 else "dve", f1[:, TT[ti]:TT[ti] + 512], pa)
            p.ts("dve", f6, f1, cwv[:, 3:4])
            p.stt("dve", f6[:, 1:2048], f1[:, 0:2047], cwv[:, 2:3], f6[:, 1:2048], ALU.mult, ALU.add)
            p.stt("dve", f6[:, 2:2048], f1[:, 0:2046], cwv[:, 1:2], f6[:, 2:2048], ALU.mult, ALU.add)
            p.stt("dve", f6[:, 3:2048], f1[:, 0:2045], cwv[:, 0:1], f6[:, 3:2048], ALU.mult, ALU.add)
            p.act(out_f32 if out_f32 is not None else out_bf, f6, AF.Silu)

        for h in range(H):
            cs = 128 * h
            p.dma("ds", cqs, SV(cq[cs:cs + 128, :], "c_cq"))
            p.dma("ds", cks, SV(ck[cs:cs + 128, :], "c_ck"))
            p.dma("ds", cvs, SV(cv[cs:cs + 128, :], "c_cv"))
            for mat, matk, slot in ((wq, "l0_wq", hwq), (wk, "l0_wk", hwk),
                                    (wv, "l0_wv", hwv), (ww, "l0_ww", hww)):
                load_wtiles("dg", slot, mat[:, cs:cs + 128], matk)
            proj4((hwq, hwk, hwv, hww), (hU, hUt, hkt, hvt))

            def conv_silu(rawb, cwv, out_bf, out_f32):
                p.ts("dve", f6, rawb, cwv[:, 3:4])
                p.stt("dve", f6[:, 1:2048], rawb[:, 0:2047], cwv[:, 2:3], f6[:, 1:2048], ALU.mult, ALU.add)
                p.stt("dve", f6[:, 2:2048], rawb[:, 0:2046], cwv[:, 1:2], f6[:, 2:2048], ALU.mult, ALU.add)
                p.stt("dve", f6[:, 3:2048], rawb[:, 0:2045], cwv[:, 0:1], f6[:, 3:2048], ALU.mult, ALU.add)
                p.act(out_f32 if out_f32 is not None else out_bf, f6, AF.Silu)
            conv_silu(hU, cqs, hq, None)
            conv_silu(hUt, cks, hk, None)
            conv_silu(hkt, cvs, hv, None)
            conv_silu(hvt, cvs, None, f1)
            for ti in range(4):
                pa = (SV(pA.ap()[:, 0:512], "pA") if ti % 2 == 0
                      else SV(pB.ap()[:, 0:512], "pB"))
                p.mm(pa, SV(sv_wf2.ap[:, cs:cs + 128], "c_wf2"), z1[:, TT[ti]:TT[ti] + 512])
                p.copy("act" if ti % 2 == 0 else "dve", cgbw[:, TT[ti]:TT[ti] + 512], pa)
            p.act(f6, cgbw, AF.Abs)
            p.act(cgate, f6, AF.Exp, scale=-1.0)
            p.act(f6, cgate, AF.Ln, bias=c_one)
            p.ts("dve", cgbw, cgbw, 0.0, op0=ALU.min)
            p.tt("dve", cgbw, cgbw, f6, ALU.subtract)
            for ch in range(16):
                a = ch * 128
                p.cumsum(cG[:, a:a + 128], cgbw[:, a:a + 128])
            p.act(dcol, SV(cG.ap[:, 127:2048:128], "cG"), AF.Exp)
            p.act(egm, SV(cG.ap[:, 63:2048:128], "cG"), AF.Exp)
            for ch in range(16):
                a = ch * 128
                p.ts("dve", cGc[:, a:a + 128], cG[:, a:a + 128],
                     SV(cG.ap[:, a + 63:a + 64], "cG"), op0=ALU.subtract)
            p.act(cG, cGc, AF.Exp, scale=-1.0)
            for ti in range(4):
                sl0, sl1 = TT[ti], TT[ti] + 512
                p.act(f6[:, sl0:sl1], f1[:, sl0:sl1], AF.Square)
                pc1 = SV(pC.ap()[0:1, 0:512], "pC")
                p.mm(pc1, c_onescol, f6[:, sl0:sl1])
                p.act(row_a, pc1, AF.Sqrt, bias=c_eps6[0:1, :])
                p.recip(row_a, row_a)
                pd1 = SV(pD.ap()[:, 0:512], "pD")
                p.mm(pd1, c_ones1, row_a)
                p.tt("dve", f6[:, sl0:sl1], f1[:, sl0:sl1], pd1, ALU.mult)
                pcr = SV(pC.ap()[0:1, 0:512], "pC")
                p.mm(pcr, SV(c_oh16.ap[:, h:h + 1], "c_oh16"),
                     SV(beta_sb.ap[:, sl0:sl1], "beta_sb"))
                p.copy("dve", row_a, pcr)
                pc2 = SV(pC.ap()[:, 0:512], "pC")
                p.mm(pc2, c_ones1, row_a)
                p.tt("dve", cgbw[:, sl0:sl1], f6[:, sl0:sl1], pc2, ALU.mult)
            p.tt("dve", hU, cgbw, cG, ALU.mult)
            for ch in range(16):
                a = ch * 128
                pg = SV(pG.ap().bitcast(BF16)[:, 0:128], "pG")
                p.tr(pg, hU[:, a:a + 128], c_idbf)
                p.copy("dve", hUt[:, a:a + 128], pg)
                ph_ = SV(pH.ap().bitcast(BF16)[:, 0:128], "pH")
                p.tr(ph_, hk[:, a:a + 128], c_idbf)
                p.copy("act", hkt[:, a:a + 128], ph_)
                pg2 = SV(pG.ap().bitcast(BF16)[:, 128:256], "pG")
                p.tr(pg2, hv[:, a:a + 128], c_idbf)
                p.copy("dve", hvt[:, a:a + 128], pg2)
            et_all = SV(ar_c.ap()[:, 6144:8192], "cgate")
            at_all = SV(sm32.ap().bitcast(BF16)[:, 1536:3584], "at_all")
            w2_all = SV(ar_d.ap()[:, 4096:8192], "f6")
            egmr = SV(ar_c.ap()[0:1, 4096:6144], "cgbw")
            dfr = SV(ar_c.ap()[0:1, 0:2048], "cG")
            for ch in range(16):
                a = ch * 128
                pd1 = (SV(pD.ap()[:, 0:128], "pD") if ch % 2 == 0
                       else SV(pD.ap()[:, 128:256], "pD"))
                p.tr(pd1, cGc[:, a:a + 128], c_idf)
                p.act(et_all[:, a:a + 128], pd1, AF.Exp)
            for ch in range(16):
                a = ch * 128
                pa1 = (SV(pA.ap()[:, 0:128], "pA") if ch % 2 == 0
                       else SV(pA.ap()[:, 128:256], "pA"))
                p.mm(pa1, hk[:, a:a + 128], hq[:, a:a + 128])
                p.tt("dve", at_all[:, a:a + 128], pa1, c_m128b, ALU.mult)
            for ch in range(16):
                a = ch * 128
                pc1 = SV(pC.ap()[0:1, 0:128], "pC")
                p.mm(pc1, c_oh127, et_all[:, a:a + 128])
                p.copy("dve", row_a128, pc1)
                pcB = SV(pC.ap()[:, 128:256], "pC")
                p.mm(pcB, c_ones1, row_a128)
                p.tt("dve", w2_all[:, a:a + 128], hUt[:, a:a + 128], pcB, ALU.mult)
                pdR = SV(pD.ap()[0:1, 256:384], "pD")
                p.tr(pdR, egm[:, ch:ch + 1], c_idf)
                p.copy("act", SV(egmr.ap[:, a:a + 128], "cgbw"), pdR)
                pdR2 = SV(pD.ap()[0:1, 384:512], "pD")
                p.tr(pdR2, dcol[:, ch:ch + 1], c_idf)
                p.copy("dve", SV(dfr.ap[:, a:a + 128], "cG"), pdR2)
            p.memset("dve", st_sk, 0.0)
            p.memset("dve", st_sv, 0.0)
            for ch in range(16):
                a = ch * 128
                pa2 = SV(pA.ap()[:, 128:256], "pA")
                pa3 = SV(pA.ap()[:, 256:384], "pA")
                pa4 = SV(pA.ap()[:, 384:512], "pA")
                pb1 = SV(pB.ap()[:, 0:128], "pB")
                pb2 = SV(pB.ap()[:, 128:256], "pB")
                pcE = SV(pC.ap()[:, 256:384], "pC")
                pcD = SV(pC.ap()[:, 384:512], "pC")
                ph1_ = SV(pH.ap().bitcast(BF16)[:, 0:128], "pH")
                p.mm(pcE, c_ones1, SV(egmr.ap[:, a:a + 128], "cgbw"))
                p.tt("dve", st_sk0s, st_sk, pcE, ALU.mult)
                p.ts("dve", st_sv0s, st_sv, egm[:, ch:ch + 1], op0=ALU.mult)
                p.mm(pa2, hq[:, a:a + 128], st_sk0s, start=True, stop=False)
                p.mm(pa2, SV(at_all.ap[:, a:a + 128], "at_all"), hUt[:, a:a + 128],
                     start=False, stop=True)
                p.tt("dve", st_tmp, pa2, et_all[:, a:a + 128], ALU.mult)
                p.red(colmx, st_tmp, "max")
                p.ts("dve", colmx2, colmx, -SCALE)
                p.act(st_tmp2, st_tmp, AF.Exp, bias=colmx2, scale=SCALE)
                p.red(colsm, st_tmp2, "sum")
                p.recip(colrc, colsm)
                p.stt("dve", st_atm, st_tmp2, colrc, et_all[:, a:a + 128], ALU.mult, ALU.mult)
                p.tr(ph1_, st_atm, c_idbf)
                p.copy("dve", st_at2, ph1_)
                p.mm(pb1, st_sv0s, st_at2, start=True, stop=False)
                p.mm(pa3, hU[:, a:a + 128], st_at2)
                p.tt("dve", st_sk0s, pa3, c_m128b, ALU.mult)
                p.mm(pb1, hvt[:, a:a + 128], st_sk0s, start=False, stop=True)
                p.copy("act", o_h[:, a:a + 128], pb1)
                p.mm(pa4, hkt[:, a:a + 128], SV(w2_all.ap[:, a:a + 128], "f6"))
                p.mm(pb2, SV(w2_all.ap[:, a:a + 128], "f6"), hvt[:, a:a + 128])
                p.stt("dve", st_sv, st_sv, dcol[:, ch:ch + 1], pb2, ALU.mult, ALU.add)
                p.mm(pcD, c_ones1, SV(dfr.ap[:, a:a + 128], "cG"))
                p.tt("dve", st_tmp, st_sk, pcD, ALU.mult)
                p.tt("dve", st_sk, st_tmp, pa4, ALU.add)
            for ti in range(4):
                pa = (SV(pA.ap()[:, 0:512], "pA") if ti % 2 == 0
                      else SV(pB.ap()[:, 0:512], "pB"))
                p.mm(pa, SV(sv_wg2.ap[:, cs:cs + 128], "c_wg2"), g1t[:, TT[ti]:TT[ti] + 512])
                p.act(SV(ar_c.ap()[:, 6144 + TT[ti]:6144 + TT[ti] + 512], "cgate"), pa,
                      AF.Sigmoid, bias=c_bg2[:, h:h + 1])
            for ti in range(4):
                sl0, sl1 = TT[ti], TT[ti] + 512
                p.act(f6[:, sl0:sl1], o_h[:, sl0:sl1], AF.Square)
                pc1 = SV(pC.ap()[0:1, 0:512], "pC")
                p.mm(pc1, c_onescol, f6[:, sl0:sl1])
                p.act(row_a, pc1, AF.Sqrt, bias=c_eps5[0:1, :], scale=1.0 / 128)
                p.recip(row_a, row_a)
                pd1 = SV(pD.ap()[:, 0:512], "pD")
                p.mm(pd1, c_ones1, row_a)
                p.stt("dve", f6[:, sl0:sl1], o_h[:, sl0:sl1], c_nw0, pd1, ALU.mult, ALU.mult)
                p.tt("dve", stg, f6[:, sl0:sl1], cgate[:, sl0:sl1], ALU.mult)
                p.dma("ds", SV(o_hbm[cs:cs + 128, sl0:sl1], "o_hbm"), stg)
        if "d_o0" in dbg_out:
            p.dma("ds", SV(dbg_out["d_o0"], "d_o0"), SV(o_hbm, "o_hbm"))
        p.barrier()

        if upto <= 2:
            p.materialize(block, sems)
            return nc, sorted(dbg_out)

        oh3 = SV(ar_b.ap()[:, 0:16384], "oh3")
        wt3 = SV(ar_b.ap()[:, 16384:18432], "wt3")
        stg3 = SV(sm32.ap()[:, 0:512], "stg3")
        for t2 in range(2):
            src = o_hbm.rearrange("(c p) t -> p c t", p=128)[:, :, t2 * 1024:(t2 + 1) * 1024]
            p.dma("ds", SV(oh3.ap.rearrange("p (c t) -> p c t", t=1024), "oh3"),
                  SV(src, "o_hbm"))
            for dc in range(16):
                wt3 = SV(ar_b.ap()[:, 16384 + (dc % 2) * 2048: 16384 + (dc % 2) * 2048 + 2048],
                         "wt3_%d" % (dc % 2))
                load_wtiles("dg", wt3, wo0[:, dc * 128:dc * 128 + 128], "l0_wo")
                for t5 in range(2):
                    pa = (SV(pA.ap()[:, 0:512], "pA") if (dc * 2 + t5) % 2 == 0
                          else SV(pB.ap()[:, 0:512], "pB"))
                    for hf in range(16):
                        p.mm(pa, wt3[:, hf * 128:hf * 128 + 128],
                             oh3[:, hf * 1024 + t5 * 512: hf * 1024 + t5 * 512 + 512],
                             start=(hf == 0), stop=(hf == 15))
                    stg3x = stg3 if t5 == 0 else SV(sm32.ap()[:, 512:1024], "stg3b")
                    p.copy("dve" if t5 == 0 else "act", stg3x, pa)
                    c0_ = t2 * 1024 + t5 * 512
                    p.dma("ds", SV(s0_hbm[dc * 128:dc * 128 + 128, c0_:c0_ + 512], "s0_hbm"), stg3x)
        if "d_s0" in dbg_out:
            p.dma("ds", SV(dbg_out["d_s0"], "d_s0"), SV(s0_hbm, "s0_hbm"))
        p.barrier()

        if upto <= 3:
            p.materialize(block, sems)
            return nc, sorted(dbg_out)

        d1 = SV(ar_b.ap()[:, 0:32768], "d1")
        for q4 in range(4):
            nq = 2048 * q4
            for nb in range(16):
                w0tl = SV(ar_d.ap()[:, (nb % 2) * 2048:(nb % 2) * 2048 + 2048], "w0tl%d" % (nb % 2))
                load_wtiles("dg", w0tl, w0t[:, nq + nb * 128: nq + nb * 128 + 128], "l0_w0t")
                bks4 = [("pA", pA), ("pB", pB), ("pE", pE), ("pF", pF)]
                for ti in range(4):
                    kname, bank = bks4[ti]
                    pa = SV(bank.ap()[:, 0:512], kname)
                    for dc in range(16):
                        p.mm(pa, w0tl[:, dc * 128:dc * 128 + 128],
                             xnv(dc, TT[ti], TT[ti] + 512), start=(dc == 0), stop=(dc == 15))
                    p.act(d1[:, nb * 2048 + TT[ti]: nb * 2048 + TT[ti] + 512], pa,
                          AF.Gelu, bias=c_b0[:, q4 * 16 + nb: q4 * 16 + nb + 1])
            for dc in range(16):
                w1tl = SV(ar_d.ap()[:, 4096 + (dc % 2) * 2048: 4096 + (dc % 2) * 2048 + 2048],
                          "w1tl%d" % (dc % 2))
                load_wtiles("dg", w1tl, w1t[nq:nq + 2048, dc * 128:dc * 128 + 128], "l0_w1t")
                bksE = [("pC", pC), ("pD", pD), ("pG", pG), ("pH", pH)]
                for ti in range(4):
                    kname, bank = bksE[ti]
                    pc_ = SV(bank.ap()[:, 0:512], kname)
                    for nb in range(16):
                        p.mm(pc_, w1tl[:, nb * 128:nb * 128 + 128],
                             d1[:, nb * 2048 + TT[ti]: nb * 2048 + TT[ti] + 512],
                             start=(nb == 0), stop=(nb == 15))
                    stgx = stg3 if ti % 2 == 0 else SV(sm32.ap()[:, 512:1024], "stg4b")
                    p.copy("dve" if ti % 2 == 0 else "act", stgx, pc_)
                    dst = SV(e_hbm[dc * 128:dc * 128 + 128, TT[ti]:TT[ti] + 512], "e_hbm")
                    if q4 == 0:
                        p.dma("ds", dst, stgx)
                    else:
                        p.dma("dg", dst, stgx, accum=True)
        if "d_e0" in dbg_out:
            p.dma("ds", SV(dbg_out["d_e0"], "d_e0"), SV(e_hbm, "e_hbm"))
        p.barrier()

        if upto <= 4:
            p.materialize(block, sems)
            return nc, sorted(dbg_out)

        ec5 = SV(sm32.ap()[:, 0:2048], "ec5")
        sq5 = SV(smbf[:, 0:512], "sq5")
        s0c5 = SV(ar_c.ap()[:, 0:2048], "s0c5")
        x0c5 = SV(ar_c.ap()[:, 2048:4096], "x0c5")
        g0c5 = SV(ar_c.ap()[:, 4096:6144], "g0c5")
        x1c5 = SV(ar_c.ap()[:, 6144:8192], "x1c5")
        mrow5 = [SV(pA.ap()[0:1, 0:512], "pA"), SV(pB.ap()[0:1, 0:512], "pB"),
                 SV(pC.ap()[0:1, 0:512], "pC"), SV(pD.ap()[0:1, 0:512], "pD")]
        vrow5 = [SV(pE.ap()[0:1, 0:512], "pE"), SV(pF.ap()[0:1, 0:512], "pF"),
                 SV(pG.ap()[0:1, 0:512], "pG"),
                 SV(pH.ap()[0:1, 0:512], "pH")]
        for dc in range(16):
            p.dma("ds", ec5, hbm_rows(e_hbm, dc * 128, dc * 128 + 128))
            p.dma("ds", s0c5, hbm_rows(s0_hbm, dc * 128, dc * 128 + 128))
            p.dma("ds", x0c5, hbm_rows(x0f, dc * 128, dc * 128 + 128, "x0f"))
            p.act(g0c5, ec5, AF.Silu, bias=c_b1[:, dc:dc + 1])
            p.tt("dve", g0c5, s0c5, g0c5, ALU.mult)
            p.tt("dve", x1c5, x0c5, g0c5, ALU.add)
            p.dma("ds", SV(x1_hbm[dc * 128:dc * 128 + 128, :], "x1_hbm"), x1c5)
            p.copy("dve", SV(ar_b.ap()[:, dc * 2048:dc * 2048 + 2048], "x1b_%d" % dc), x1c5)
            for ti in range(4):
                xt = x1c5[:, TT[ti]:TT[ti] + 512]
                p.mm(mrow5[ti], c_onescol, xt, start=(dc == 0), stop=(dc == 15))
                p.act(sq5, xt, AF.Square)
                p.mm(vrow5[ti], c_onescol, sq5, start=(dc == 0), stop=(dc == 15))
        if "d_x1" in dbg_out:
            p.dma("ds", SV(dbg_out["d_x1"], "d_x1"), SV(x1_hbm, "x1_hbm"))
        p.barrier()
        mrow5 = [SV(pA.ap()[0:1, 0:512], "pA"), SV(pB.ap()[0:1, 0:512], "pB"),
                 SV(pC.ap()[0:1, 0:512], "pC"), SV(pD.ap()[0:1, 0:512], "pD")]
        vrow5 = [SV(pE.ap()[0:1, 0:512], "pE"), SV(pF.ap()[0:1, 0:512], "pF"),
                 SV(pG.ap()[0:1, 0:512], "pG"),
                 SV(pH.ap()[0:1, 0:512], "pH")]
        rowm = SV(ar_c.ap()[0:1, 5120:5632], "rowm")
        rowv = SV(ar_c.ap()[0:1, 4096:4608], "rowv")
        rowr = SV(ar_c.ap()[0:1, 4608:5120], "rowr")
        for ti in range(4):
            p.ts("dve", rowm, mrow5[ti], 1.0 / D)
            p.ts("dve", rowv, vrow5[ti], 1.0 / D)
            p.act(rowr, rowm, AF.Square)
            p.tt("dve", rowv, rowv, rowr, ALU.subtract)
            p.act(rowr, rowv, AF.Sqrt, bias=c_eps5[0:1, :])
            p.recip(rowr, rowr)
            pa = SV(pA.ap()[:, 0:512], "pA")
            pb = SV(pE.ap()[:, 0:512], "pE")
            p.mm(pa, c_ones1, rowm)
            p.mm(pb, c_ones1, rowr)
            p.copy("dve", SV(ar_c.ap()[:, TT[ti]:TT[ti] + 512], "mB1"), pa)
            p.copy("act", SV(ar_c.ap()[:, 2048 + TT[ti]:2048 + TT[ti] + 512], "rB1"), pb)
        p.barrier()
        mB1 = SV(ar_c.ap()[:, 0:2048], "mB1")
        rB1 = SV(ar_c.ap()[:, 2048:4096], "rB1")
        sub5 = SV(ar_c.ap()[:, 4096:6144], "sub5")
        for dc in range(16):
            xb = SV(ar_b.ap()[:, dc * 2048:dc * 2048 + 2048], "x1b_%d" % dc)
            p.tt("dve", sub5, xb, mB1, ALU.subtract)
            p.tt("dve", xnv(dc, 0, 2048), sub5, rB1, ALU.mult)
            p.dma("ds", SV(xn1_hbm[dc * 128:dc * 128 + 128, :], "xn1_hbm"), xnv(dc, 0, 2048))
        if "d_xn1" in dbg_out:
            p.dma("ds", SV(dbg_out["d_xn1"], "d_xn1"), SV(xn1_hbm, "xn1_hbm"))
        p.barrier()

        if upto <= 5:
            p.materialize(block, sems)
            return nc, sorted(dbg_out)

        kp6 = SV(ar_b.ap()[:, 0:2048], "kp6")
        vp6 = SV(ar_b.ap()[:, 2048:4096], "vp6")
        kt6 = SV(ar_b.ap()[:, 4096:8192], "kt6")
        vt6 = SV(ar_b.ap()[:, 8192:12288], "vt6")
        qp6 = SV(ar_b.ap()[:, 12288:14336], "qp6")
        kraw = SV(ar_b.ap()[:, 14336:16384], "kraw")
        qraw = SV(ar_b.ap()[:, 18432:20480], "qraw")
        wsl6s = [SV(ar_b.ap()[:, 16384:18432], "wsl6a"),
                 SV(ar_b.ap()[:, 20480:22528], "wsl6b")]
        fm1 = SV(d32[:, 0:2048], "fm1")
        fm2 = SV(d32[:, 2048:4096], "fm2")
        o6 = SV(d32[:, 4096:6144], "o6")
        kv_sb = SV(smb.ap()[:, 0:128], "kv_sb")
        sm32b = sm32.ap().bitcast(BF16)
        ones512b = SV(sm32b[0:1, 0:512], "ones512b")
        b_fq1 = SV(sm32b[0:1, 512:640], "b_fq1")
        b_fq2 = SV(sm32b[0:1, 640:768], "b_fq2")
        b_fk1 = SV(sm32b[0:1, 768:896], "b_fk1")
        b_fk2 = SV(sm32b[0:1, 896:1024], "b_fk2")
        p.memset("dve", ones512b, 1.0)
        p.dma("dg", b_fq1, fq1b)
        p.dma("dg", b_fq2, fq2b)
        p.dma("dg", b_fk1, fk1b)
        p.dma("dg", b_fk2, fk2b)
        stm6 = SV(smb.ap()[0:64, 128:192], "stm6")
        stg6 = SV(smb.ap()[:, 640:1152], "stg")

        def proj6(matk, mat, cs, dst, nt, wi):
            wsl = wsl6s[wi % 2]
            load_wtiles("dg", wsl, mat[:, cs:cs + 128], matk)
            bks6 = [("pA", pA), ("pB", pB), ("pE", pE), ("pF", pF)]
            for ti in range(nt):
                kname, bank = bks6[ti % 4]
                pa = SV(bank.ap()[:, 0:512], kname)
                for dc in range(16):
                    p.mm(pa, wsl[:, dc * 128:dc * 128 + 128],
                         xnv(dc, TT[ti], TT[ti] + 512), start=(dc == 0), stop=(dc == 15))
                p.copy("act" if ti % 2 == 0 else "dve", dst[:, TT[ti]:TT[ti] + 512], pa)

        def fmap6(srcb, w1c, b1c_, w2c, b2c_, dst, nt, scaled):
            for ti in range(nt):
                sl0, sl1 = TT[ti], TT[ti] + 512
                pd_ = SV(pD.ap()[:, 0:512], "pD")
                p.mm(pd_, w1c, srcb[:, sl0:sl1], start=True, stop=False)
                p.mm(pd_, b1c_, ones512b, start=False, stop=True)
                p.copy("act", fm1[:, sl0:sl1], pd_)
                pe_ = SV(pF.ap()[:, 0:512], "pF")
                p.mm(pe_, w2c, srcb[:, sl0:sl1], start=True, stop=False)
                p.mm(pe_, b2c_, ones512b, start=False, stop=True)
                p.copy("dve", fm2[:, sl0:sl1], pe_)
                if scaled:
                    p.stt("dve", dst[:, sl0:sl1], fm1[:, sl0:sl1], SCALE,
                          fm2[:, sl0:sl1], ALU.mult, ALU.mult)
                else:
                    p.tt("dve", dst[:, sl0:sl1], fm1[:, sl0:sl1], fm2[:, sl0:sl1], ALU.mult)

        for h in range(H):
            cs = 128 * h
            proj6("l1_wqt", wqt1, cs, qraw, 4, 0)
            fmap6(qraw, c_fq1t, b_fq1, c_fq2t, b_fq2, qp6, 4, True)
            proj6("l1_wkt", wkt1, cs, kraw, 4, 1)
            fmap6(kraw, c_fk1t, b_fk1, c_fk2t, b_fk2, kp6, 4, False)
            proj6("l1_wvt", wvt1, cs, vp6, 4, 0)
            for tc in range(32):
                a = tc * 64
                pg = SV(pG.ap().bitcast(BF16)[0:64, 0:128], "pG")
                p.tr(pg, kp6[:, a:a + 64], c_idbf)
                p.copy("dve", SV(kt6.ap[0:64, tc * 128:tc * 128 + 128], "kt6"), pg)
                ph_ = SV(pH.ap().bitcast(BF16)[0:64, 0:128], "pH")
                p.tr(ph_, vp6[:, a:a + 64], c_idbf)
                p.copy("act", SV(vt6.ap[0:64, tc * 128:tc * 128 + 128], "vt6"), ph_)
            p.memset("dve", kv_sb, 0.0)
            for cc in range(32):
                a64 = cc * 64
                ktv = SV(kt6.ap[0:64, cc * 128:cc * 128 + 128], "kt6")
                vtv = SV(vt6.ap[0:64, cc * 128:cc * 128 + 128], "vt6")
                pa1 = SV(pA.ap()[:, 0:64], "pA")
                p.mm(pa1, kv_sb, qp6[:, a64:a64 + 64], start=True, stop=False)
                pb1 = SV(pB.ap()[0:64, 0:64], "pB")
                p.mm(pb1, kp6[:, a64:a64 + 64], qp6[:, a64:a64 + 64])
                p.tt("dve", stm6, pb1, c_m64, ALU.mult)
                p.mm(pa1, vtv, stm6, start=False, stop=True)
                p.copy("act", o6[:, a64:a64 + 64], pa1)
                pe1 = SV(pE.ap()[:, 0:128], "pE")
                p.mm(pe1, ktv, vtv, start=(cc == 0), stop=(cc == 31))
                if cc < 31:
                    p.copy("dve", kv_sb, pe1)
            for ti in range(4):
                sl0, sl1 = TT[ti], TT[ti] + 512
                p.act(fm1[:, sl0:sl1], o6[:, sl0:sl1], AF.Square)
                pc1 = SV(pC.ap()[0:1, 0:512], "pC")
                p.mm(pc1, c_onescol, fm1[:, sl0:sl1])
                p.act(row_a, pc1, AF.Sqrt, bias=c_eps5[0:1, :], scale=1.0 / 128)
                p.recip(row_a, row_a)
                pd1 = SV(pD.ap()[:, 0:512], "pD")
                p.mm(pd1, c_ones1, row_a)
                p.stt("dve", stg6, o6[:, sl0:sl1], c_nw1, pd1, ALU.mult, ALU.mult)
                p.dma("ds", SV(o_hbm[cs:cs + 128, sl0:sl1], "o1_hbm"), stg6)
        if "d_o1" in dbg_out:
            p.dma("ds", SV(dbg_out["d_o1"], "d_o1"), SV(o_hbm, "o1_hbm"))
        p.barrier()

        if upto <= 6:
            p.materialize(block, sems)
            return nc, sorted(dbg_out)

        oh7 = SV(ar_b.ap()[:, 0:16384], "oh7")
        wt7 = SV(ar_b.ap()[:, 16384:18432], "wt7")
        stg7 = SV(sm32.ap()[:, 0:512], "stg7")
        for t2 in range(2):
            src = o_hbm.rearrange("(c p) t -> p c t", p=128)[:, :, t2 * 1024:(t2 + 1) * 1024]
            p.dma("ds", SV(oh7.ap.rearrange("p (c t) -> p c t", t=1024), "oh7"),
                  SV(src, "o1_hbm"))
            for dc in range(16):
                wt7 = SV(ar_b.ap()[:, 16384 + (dc % 2) * 2048: 16384 + (dc % 2) * 2048 + 2048],
                         "wt7_%d" % (dc % 2))
                load_wtiles("dg", wt7, wot1[:, dc * 128:dc * 128 + 128], "l1_wot")
                for t5 in range(2):
                    pa = (SV(pA.ap()[:, 0:512], "pA") if (dc * 2 + t5) % 2 == 0
                          else SV(pB.ap()[:, 0:512], "pB"))
                    for hf in range(16):
                        p.mm(pa, wt7[:, hf * 128:hf * 128 + 128],
                             oh7[:, hf * 1024 + t5 * 512: hf * 1024 + t5 * 512 + 512],
                             start=(hf == 0), stop=(hf == 15))
                    stg7x = stg7 if t5 == 0 else SV(sm32.ap()[:, 512:1024], "stg7b")
                    p.copy("dve" if t5 == 0 else "act", stg7x, pa)
                    c0_ = t2 * 1024 + t5 * 512
                    p.dma("ds", SV(s1_hbm[dc * 128:dc * 128 + 128, c0_:c0_ + 512], "s1_hbm"), stg7x)
        if "d_s1" in dbg_out:
            p.dma("ds", SV(dbg_out["d_s1"], "d_s1"), SV(s1_hbm, "s1_hbm"))
        p.barrier()

        if upto <= 7:
            p.materialize(block, sems)
            return nc, sorted(dbg_out)

        xno = SV(ar_xn.ap()[:, 0:16384], "xno")
        for dc in range(16):
            p.ts("dve", idxw, c_jidx, dc * 128 * T, None, ALU.add)
            p.dma_gather(SV(ar_xn.ap()[:, dc * 1024:dc * 1024 + 1024], "xno"),
                         hbm_rows(xn1_hbm, 0, 128, "xn1_hbm"), idxw)
        interblk = SV(ar_b.ap()[:, 0:4096], "interblk")
        gdqs = [SV(ar_b.ap()[:, 8192:16384], "gdq0"), SV(ar_b.ap()[:, 16384:24576], "gdq1")]
        tgb = SV(smbi[:, 0:512], "tgb")
        lo16 = SV(smbi[:, 512:1024], "lo16")
        hi16 = SV(smbi[:, 1024:1536], "hi16")
        sg16 = SV(smbi[:, 1536:2048], "sg16")
        lo_bf = SV(smb.ap()[:, 512:1024], "lo16")
        hi_bf = SV(smb.ap()[:, 1024:1536], "hi16")
        scsl = SV(sm32.ap()[0:8, 0:512], "scsl")
        gblk = SV(ar_c.ap()[:, 0:4096], "gblk")
        g2a = SV(sm32.ap()[:, 1024:1536], "g2a")
        g2b = SV(sm32.ap()[:, 1536:2048], "g2b")

        def dequant8(packed, packedk, scales, scalesk, nb0, gdq):
            for qc in range(8):
                p.dma("dg", tgb,
                      SV(packed[qc * 128:qc * 128 + 128, nb0:nb0 + 512], packedk))
                p.ts("dve", lo16, tgb, 7, 6, ALU.bitwise_and, ALU.logical_shift_left)
                p.ts("dve", sg16, tgb, 8, 12, ALU.bitwise_and, ALU.logical_shift_left)
                p.tt("dve", lo16, lo16, sg16, ALU.bitwise_or)
                p.ts("dve", hi16, tgb, 0x70, 2, ALU.bitwise_and, ALU.logical_shift_left)
                p.ts("dve", sg16, tgb, 0x80, 8, ALU.bitwise_and, ALU.logical_shift_left)
                p.tt("dve", hi16, hi16, sg16, ALU.bitwise_or)
                p.dma("ds", scsl, SV(scales[qc * 8:qc * 8 + 8, nb0:nb0 + 512], scalesk))
                pcx = SV(pC.ap()[:, 0:512], "pC")
                p.mm(pcx, c_pexp8, scsl)
                scb = SV(smb.ap()[:, 2048:2560], "scb8")
                p.act(scb, pcx, AF.Copy)
                base = qc * 1024
                p.stt("dve", gdq[:, base:base + 512], lo_bf, P126, scb, ALU.mult, ALU.mult)
                p.stt("dve", gdq[:, base + 512:base + 1024], hi_bf, P126, scb, ALU.mult, ALU.mult)

        def gu_mms(gdq, banks, dst_fn):
            for nb in range(4):
                for t5 in range(2):
                    pa = banks[(nb * 2 + t5) % 2]
                    kk = 0
                    for qc in range(8):
                        for par in range(2):
                            p.mm(pa, gdq[:, qc * 1024 + par * 512 + nb * 128:
                                         qc * 1024 + par * 512 + nb * 128 + 128],
                                 SV(ar_xn.ap()[:, (2 * qc + par) * 1024 + t5 * 512:
                                               (2 * qc + par) * 1024 + t5 * 512 + 512], "xno"),
                                 start=(kk == 0), stop=(kk == 15))
                            kk += 1
                    dst_fn(nb, t5, pa)

        bkAB = [SV(pA.ap()[:, 0:512], "pA"), SV(pB.ap()[:, 0:512], "pB")]
        bkEF = [SV(pE.ap()[:, 0:512], "pE"), SV(pF.ap()[:, 0:512], "pF")]
        for nblk in range(32):
            nb0 = nblk * 512
            g_ = gdqs[nblk % 2]
            dequant8(tg, "l1_tg", gsc, "l1_gsc", nb0, g_)

            def st_gate(nb, t5, pa):
                p.copy("act" if t5 == 0 else "dve",
                       gblk[:, nb * 1024 + t5 * 512: nb * 1024 + t5 * 512 + 512], pa)
            gu_mms(g_, bkAB, st_gate)
            dequant8(tu, "l1_tu", usc, "l1_usc", nb0, g_)

            def st_up(nb, t5, pb):
                gc = gblk[:, nb * 1024 + t5 * 512: nb * 1024 + t5 * 512 + 512]
                p.act(g2a, gc, AF.Square)
                p.ts("dve", g2a, g2a, GELT_B, GELT_A, op0=ALU.mult, op1=ALU.add)
                p.tt("dve", g2b, g2a, gc, ALU.mult)
                p.act(g2a, g2b, AF.Tanh)
                p.ts("dve", g2a, g2a, 0.5, 0.5, op0=ALU.mult, op1=ALU.add)
                p.tt("dve", g2b, g2a, gc, ALU.mult)
                p.tt("dve", interblk[:, nb * 1024 + t5 * 512: nb * 1024 + t5 * 512 + 512],
                     g2b, pb, ALU.mult)
            gu_mms(g_, bkEF, st_up)
            dstint = inter_hbm[nb0:nb0 + 512, :].rearrange("(c p) t -> p c t", p=128)
            p.dma("ds", SV(dstint, "inter_hbm"),
                  SV(interblk.ap.rearrange("p (c t) -> p c t", t=1024), "interblk"))
        if "d_inter" in dbg_out:
            p.dma("ds", SV(dbg_out["d_inter"], "d_inter"), SV(inter_hbm, "inter_hbm"))
        p.barrier()

        if upto <= 80:
            p.materialize(block, sems)
            return nc, sorted(dbg_out)

        bi16 = ar_b.ap().bitcast(I16)
        dqL = [SV(ar_b.ap()[:, 0:512], "dqL0"), SV(ar_b.ap()[:, 1024:1536], "dqL1")]
        dqH = [SV(ar_b.ap()[:, 512:1024], "dqH0"), SV(ar_b.ap()[:, 1536:2048], "dqH1")]
        ibL = [SV(ar_b.ap()[:, 2048:3072], "ibL0"), SV(ar_b.ap()[:, 4096:5120], "ibL1")]
        ibH = [SV(ar_b.ap()[:, 3072:4096], "ibH0"), SV(ar_b.ap()[:, 5120:6144], "ibH1")]
        tdbs = [SV(bi16[:, 6144:6656], "tdb0"), SV(bi16[:, 6656:7168], "tdb1")]
        lo16d = [SV(bi16[:, 7168:7680], "lo0"), SV(bi16[:, 7680:8192], "lo1")]
        hi16d = [SV(bi16[:, 8192:8704], "hi0"), SV(bi16[:, 8704:9216], "hi1")]
        sgLd = [SV(bi16[:, 9216:9728], "sgL0"), SV(bi16[:, 9728:10240], "sgL1")]
        sgHd = [SV(bi16[:, 10240:10752], "sgH0"), SV(bi16[:, 10752:11264], "sgH1")]
        lo_bfd = [SV(ar_b.ap()[:, 7168:7680], "lo0"), SV(ar_b.ap()[:, 7680:8192], "lo1")]
        hi_bfd = [SV(ar_b.ap()[:, 8192:8704], "hi0"), SV(ar_b.ap()[:, 8704:9216], "hi1")]
        scds = [SV(sm32.ap()[:, 0:512], "scd0"), SV(sm32.ap()[:, 512:1024], "scd1")]
        stg8s = [SV(sm32.ap()[:, 1024:1536], "stg8a"), SV(sm32.ap()[:, 1536:2048], "stg8b")]
        banks8 = [SV(pA.ap()[:, 0:512], "pA"), SV(pB.ap()[:, 0:512], "pB"),
                  SV(pC.ap()[:, 0:512], "pC"), SV(pD.ap()[:, 0:512], "pD"),
                  SV(pE.ap()[:, 0:512], "pE"), SV(pF.ap()[:, 0:512], "pF"),
                  SV(pG.ap()[:, 0:512], "pG"),
                  SV(pH.ap()[:, 0:512], "pH")]
        for grp in range(4):
            dg0 = grp * 512
            for qn in range(64):
                pr = qn % 2
                tdb_, lo_, hi_, sgL_, sgH_ = tdbs[pr], lo16d[pr], hi16d[pr], sgLd[pr], sgHd[pr]
                p.dma("dg", tdb_, SV(td[qn * 128:qn * 128 + 128, dg0:dg0 + 512], "l1_td"))
                p.ts("dve", lo_, tdb_, 7, 6, ALU.bitwise_and, ALU.logical_shift_left)
                p.ts("dve", sgL_, tdb_, 8, 12, ALU.bitwise_and, ALU.logical_shift_left)
                p.tt("dve", lo_, lo_, sgL_, ALU.bitwise_or)
                p.ts("dve", hi_, tdb_, 0x70, 2, ALU.bitwise_and, ALU.logical_shift_left)
                p.ts("dve", sgH_, tdb_, 0x80, 8, ALU.bitwise_and, ALU.logical_shift_left)
                p.tt("dve", hi_, hi_, sgH_, ALU.bitwise_or)
                p.dma("ds", scds[pr], SV(dscx[qn * 128:qn * 128 + 128, dg0:dg0 + 512], "l1_dscx"))
                p.stt("dve", dqL[pr], lo_bfd[pr], P126, scds[pr], ALU.mult, ALU.mult)
                p.stt("dve", dqH[pr], hi_bfd[pr], P126, scds[pr], ALU.mult, ALU.mult)
                p.dma("ds", ibL[pr], hbm_rows(inter_hbm, (2 * qn) * 128, (2 * qn) * 128 + 128, "inter_hbm"))
                p.dma("ds", ibH[pr], hbm_rows(inter_hbm, (2 * qn + 1) * 128, (2 * qn + 1) * 128 + 128, "inter_hbm"))
                for ds_ in range(4):
                    for t5 in range(2):
                        bk = banks8[ds_ * 2 + t5]
                        p.mm(bk, dqL[pr][:, ds_ * 128:ds_ * 128 + 128], ibL[pr][:, t5 * 512:t5 * 512 + 512],
                             start=(qn == 0), stop=False)
                        p.mm(bk, dqH[pr][:, ds_ * 128:ds_ * 128 + 128], ibH[pr][:, t5 * 512:t5 * 512 + 512],
                             start=False, stop=(qn == 63))
            for ds_ in range(4):
                for t5 in range(2):
                    sx = stg8s[(ds_ * 2 + t5) % 2]
                    p.copy("dve" if (ds_ % 2 == 0) else "act", sx, banks8[ds_ * 2 + t5])
                    r0 = dg0 + ds_ * 128
                    p.dma("ds", SV(down_hbm[r0:r0 + 128, t5 * 512:t5 * 512 + 512], "down_hbm"), sx)
        if "d_down" in dbg_out:
            p.dma("ds", SV(dbg_out["d_down"], "d_down"), SV(down_hbm, "down_hbm"))
        p.barrier()

        if upto <= 81:
            p.materialize(block, sems)
            return nc, sorted(dbg_out)

        dn9 = SV(sm32.ap()[:, 0:1024], "dn9")
        g19 = SV(sm32.ap()[:, 1024:2048], "g19")
        s19 = SV(ar_c.ap()[:, 0:1024], "s19")
        x19 = SV(ar_c.ap()[:, 1024:2048], "x19")
        for dc in range(16):
            p.dma("ds", dn9, hbm_rows(down_hbm, dc * 128, dc * 128 + 128, "down_hbm"))
            p.act(g19, dn9, AF.Silu)
            p.ts("dve", idxw, c_jidx, dc * 128 * T, None, ALU.add)
            p.dma_gather(s19, hbm_rows(s1_hbm, 0, 128, "s1_hbm"), idxw)
            p.dma_gather(x19, hbm_rows(x1_hbm, 0, 128, "x1_hbm"), idxw)
            p.tt("dve", g19, s19, g19, ALU.mult)
            p.tt("dve", x19, x19, g19, ALU.add)
            p.dma("ds", SV(out[dc * 128:dc * 128 + 128, :], "out"), x19)

        import os
        if os.environ.get('KDBG'):
            global _LAST_PROG
            _LAST_PROG = p
        p.materialize(block, sems)
    return nc, sorted(dbg_out)



def _consts():
    c = {}
    c["ident_bf"] = np.eye(128, dtype=ml_dtypes.bfloat16)
    c["ident_f32"] = np.eye(128, dtype=np.float32)
    sc = np.arange(128)
    c["mask128"] = (sc[:, None] <= sc[None, :]).astype(np.float32)
    s64 = np.arange(64)
    c["mask64"] = (s64[:, None] <= s64[None, :]).astype(np.float32)
    c["ones1"] = np.ones((1, 128), np.float32)
    c["onescol"] = np.ones((128, 1), np.float32)
    oh = np.zeros((128, 1), np.float32)
    oh[127, 0] = 1.0
    c["oh127"] = oh
    c["oh16"] = np.eye(16, dtype=np.float32)
    c["pexp8"] = (np.arange(8)[:, None] == (np.arange(128)[None, :] // 16)).astype(np.float32)
    return c


def _prep_shared(i):
    f = np.float32
    A = np.ascontiguousarray
    d = {}
    d["l0_wq"] = A(np.asarray(i["L0_attn_Wq"], f)[PI_D])
    d["l0_wk"] = A(np.asarray(i["L0_attn_Wk"], f)[PI_D])
    d["l0_wv"] = A(np.asarray(i["L0_attn_Wv"], f)[PI_D])
    d["l0_ww"] = A(np.asarray(i["L0_attn_Ww"], f)[PI_D])
    d["l0_cq"] = A(np.asarray(i["L0_attn_cq"], f))
    d["l0_ck"] = A(np.asarray(i["L0_attn_ck"], f))
    d["l0_cv"] = A(np.asarray(i["L0_attn_cv"], f))
    d["l0_wf1"] = A(np.asarray(i["L0_attn_Wf1"], f)[PI_D])
    d["l0_wf2"] = A(np.asarray(i["L0_attn_Wf2"], f))
    d["l0_wb"] = A(np.asarray(i["L0_attn_Wb"], f)[PI_D])
    d["l0_wg1"] = A(np.asarray(i["L0_attn_Wg1"], f)[PI_D])
    d["l0_wg2"] = A(np.asarray(i["L0_attn_Wg2"], f))
    d["l0_bg2"] = A(np.asarray(i["L0_attn_bg2"], f).reshape(16, 128).T)
    d["l0_normw"] = A(np.asarray(i["L0_attn_norm_w"], f).reshape(128, 1))
    d["l0_wo"] = A(np.asarray(i["L0_attn_Wo"], f)[:, PI_D])
    d["l0_w0t"] = A(np.asarray(i["L0_ffn_W0"], f).T[PI_D])
    d["l0_b0"] = A(np.asarray(i["L0_ffn_bias0"], f).reshape(64, 128).T)
    d["l0_w1t"] = A(np.asarray(i["L0_ffn_W1"], f).T[:, PI_D])
    d["l0_b1"] = A(np.asarray(i["L0_ffn_bias1"], f)[PI_D].reshape(16, 128).T)
    d["l1_wqt"] = A(np.asarray(i["L1_attn_wq"], f).T[PI_D])
    d["l1_wkt"] = A(np.asarray(i["L1_attn_wk"], f).T[PI_D])
    d["l1_wvt"] = A(np.asarray(i["L1_attn_wv"], f).T[PI_D])
    d["l1_wot"] = A(np.asarray(i["L1_attn_wo"], f).T[:, PI_D])
    d["l1_fq1t"] = A(np.asarray(i["L1_attn_fq1_w"], f).T)
    d["l1_fq2t"] = A(np.asarray(i["L1_attn_fq2_w"], f).T)
    d["l1_fk1t"] = A(np.asarray(i["L1_attn_fk1_w"], f).T)
    d["l1_fk2t"] = A(np.asarray(i["L1_attn_fk2_w"], f).T)
    d["l1_fq1b"] = A(np.asarray(i["L1_attn_fq1_b"], f).reshape(1, 128))
    d["l1_fq2b"] = A(np.asarray(i["L1_attn_fq2_b"], f).reshape(1, 128))
    d["l1_fk1b"] = A(np.asarray(i["L1_attn_fk1_b"], f).reshape(1, 128))
    d["l1_fk2b"] = A(np.asarray(i["L1_attn_fk2_b"], f).reshape(1, 128))
    d["l1_normw"] = A(np.asarray(i["L1_attn_norm_w"], f).reshape(128, 1))
    d["l1_tg"] = A(np.asarray(i["L1_ffn_gate_W_packed"], np.int32)[PI_N].T)
    d["l1_tu"] = A(np.asarray(i["L1_ffn_up_W_packed"], np.int32)[PI_N].T)
    d["l1_gsc"] = A(np.asarray(i["L1_ffn_gate_scales"], f)[PI_N].T)
    d["l1_usc"] = A(np.asarray(i["L1_ffn_up_scales"], f)[PI_N].T)
    d["l1_td"] = A(np.asarray(i["L1_ffn_down_W_packed"], np.int32).T[:, PI_D])
    d["l1_dscx"] = A(np.asarray(i["L1_ffn_down_scales"], f).T[np.arange(8192) // 16][:, PI_D])
    d.update(_consts())
    return d


def make_in_maps(inputs, n_cores=8):
    shared = _prep_shared(inputs)
    x0 = np.asarray(inputs["L0_attn_hidden_states"], np.float32)
    maps = []
    for c in range(n_cores):
        b_, j = c // 2, c % 2
        m = dict(shared)
        m["x0f"] = np.ascontiguousarray(x0[b_].T[PI_D])
        m["jidx"] = (np.arange(128)[:, None] * T + j * TH).astype(np.int32)
        maps.append(m)
    return maps


def assemble(results):
    out = np.empty((B, T, D), np.float32)
    for c, r in enumerate(results):
        b_, j = c // 2, c % 2
        out[b_, j * TH:(j + 1) * TH, :] = np.asarray(r["out"], np.float32).T[:, INV_PI_D]
    return out


_NC_CACHE = {}


def kernel(**inputs):
    key = "full"
    if key not in _NC_CACHE:
        _NC_CACHE[key] = build()
    nc, _ = _NC_CACHE[key]
    in_maps = make_in_maps(inputs)
    res = run_bass_kernel_spmd(nc, in_maps, core_ids=list(range(8)))
    return assemble(res.results)


# version 12
# speedup vs baseline: 1.0255x; 1.0255x over previous
import bisect
import numpy as np
import ml_dtypes
import concourse.bass as bass
import concourse.mybir as mybir
from concourse.bass_utils import run_bass_kernel_spmd
from contextlib import ExitStack

F32 = mybir.dt.float32
BF16 = mybir.dt.bfloat16
I32 = mybir.dt.int32
I16 = mybir.dt.int16
AF = mybir.ActivationFunctionType
ALU = mybir.AluOpType
AXX = mybir.AxisListType.X

B, T, D, H, DK = 4, 2048, 2048, 16, 128
TH = T // 2
NC1 = 16384
SCALE = float(DK ** -0.5)
P126 = float(2.0 ** 126)
GELT_A = 0.7978845608028654
GELT_B = 0.044715 * GELT_A


def perm_pi(n):
    out = np.empty(n, np.int64)
    idx = np.arange(n).reshape(n // 256, 256)
    for b_ in range(n // 256):
        out[b_ * 256:b_ * 256 + 128] = idx[b_][0::2]
        out[b_ * 256 + 128:b_ * 256 + 256] = idx[b_][1::2]
    return out


PI_D = perm_pi(D)
PI_N = perm_pi(NC1)
INV_PI_D = np.argsort(PI_D)



class SV:
    __slots__ = ("ap", "key")

    def __init__(self, ap, key):
        self.ap = ap
        self.key = key

    def __getitem__(self, sl):
        return SV(self.ap[sl], self.key)


def _ap(x):
    return x.ap if isinstance(x, SV) else x


def _k(x):
    if isinstance(x, SV):
        return x.key
    t = getattr(x, "tensor", None)
    if t is not None and hasattr(t, "name"):
        return t.name
    if hasattr(x, "name"):
        return str(x.name)
    return str(id(x))


ENGS = ("pe", "dve", "act", "pool", "sp")


class Rec:
    __slots__ = ("eng", "fn", "reads", "writes", "seq", "waits", "inc", "iss",
                 "tag", "sync_out")


class Prog:

    NDMA = 10

    def __init__(self, nc):
        self.nc = nc
        self.recs = []
        self.dsubs = tuple("dg%d" % i for i in range(self.NDMA)) + \
                     tuple("ds%d" % i for i in range(self.NDMA))
        self.counts = {e: 0 for e in ENGS + self.dsubs}
        self.rr = {"dg": 0, "ds": 0}
        self.last_w = {}
        self.readers = {}

    def _record(self, eng, fn, reads, writes, iss=None, tag="", sync_out=False):
        r = Rec()
        r.eng, r.fn, r.tag = eng, fn, tag
        r.sync_out = sync_out
        r.iss = iss or eng
        r.reads = [_k(a) for a in reads]
        r.writes = [_k(a) for a in writes]
        self.counts[eng] += 1
        r.seq = self.counts[eng]
        deps = {}

        def need(e2, s2, force=False):
            if e2 == eng and e2 in ENGS and not force:
                return
            if deps.get(e2, 0) < s2:
                deps[e2] = s2

        for b_ in r.reads:
            lw = self.last_w.get(b_)
            if lw:
                need(lw[0], lw[1], force=(len(lw) > 2 and lw[2]))
        for b_ in r.writes:
            lw = self.last_w.get(b_)
            if lw:
                need(lw[0], lw[1])
            for e2, s2 in self.readers.get(b_, {}).items():
                need(e2, s2)
        r.waits = sorted(deps.items())
        for b_ in r.writes:
            self.last_w[b_] = (eng, r.seq, sync_out)
            self.readers[b_] = {}
        for b_ in r.reads:
            d = self.readers.setdefault(b_, {})
            if d.get(eng, 0) < r.seq:
                d[eng] = r.seq
        self.recs.append(r)
        return r

    def barrier(self):
        cur = dict(self.counts)
        for e in ENGS:
            def fn(h):
                return h.nop()
            r = Rec()
            r.eng, r.iss, r.fn, r.tag = e, e, fn, "barrier"
            r.reads, r.writes = [], []
            self.counts[e] += 1
            r.seq = self.counts[e]
            r.waits = [(e2, s) for e2, s in cur.items() if e2 != e and s > 0]
            self.recs.append(r)
        self.last_w = {}
        self.readers = {}

    def _dma_sub(self, queue):
        sub = "%s%d" % (queue, self.rr[queue] % self.NDMA)
        self.rr[queue] += 1
        return sub

    def dma(self, queue, out, in_, accum=False, tag=""):
        iss = "pool" if queue == "dg" else "sp"
        sub = self._dma_sub(queue)
        o, i = _ap(out), _ap(in_)
        kw = {}
        if accum:
            kw["accum_op"] = ALU.add

        def fn(h):
            return h.dma_start(out=o, in_=i, **kw)
        rd = [in_] + ([out] if accum else [])
        r = self._record(sub, fn, rd, [out], iss=iss, tag=tag)
        if r.seq > 1:
            d = dict(r.waits)
            if d.get(sub, 0) < r.seq - 1:
                d[sub] = r.seq - 1
            r.waits = sorted(d.items())
        return r

    def mm(self, out, lhsT, rhs, start=True, stop=True, tag=""):
        o, l_, r_ = _ap(out), _ap(lhsT), _ap(rhs)

        def fn(h):
            return h.matmul(o, l_, r_, start=start, stop=stop)
        return self._record("pe", fn, [lhsT, rhs], [out], tag=tag)

    def tr(self, out, in_, ident, tag=""):
        o, i, d = _ap(out), _ap(in_), _ap(ident)

        def fn(h):
            return h.transpose(o, i, d)
        return self._record("pe", fn, [in_, ident], [out], tag=tag)

    def tt(self, eng, out, a, b_, op, tag=""):
        o, x, y = _ap(out), _ap(a), _ap(b_)

        def fn(h):
            return h.tensor_tensor(o, x, y, op)
        return self._record(eng, fn, [a, b_], [out], tag=tag)

    def ts(self, eng, out, a, s1, s2=None, op0=ALU.mult, op1=ALU.bypass, tag=""):
        o, x = _ap(out), _ap(a)
        v1 = _ap(s1) if isinstance(s1, SV) else s1
        v2 = _ap(s2) if isinstance(s2, SV) else s2
        rd = [a] + [s for s in (s1, s2) if isinstance(s, SV) or hasattr(s, "tensor")]

        def fn(h):
            if s2 is None:
                return h.tensor_single_scalar(o, x, v1, op0)
            return h.tensor_scalar(o, x, v1, v2, op0, op1)
        return self._record(eng, fn, rd, [out], tag=tag)

    def stt(self, eng, out, in0, scalar, in1, op0, op1, tag=""):
        o, x, y = _ap(out), _ap(in0), _ap(in1)
        sc = _ap(scalar) if isinstance(scalar, SV) else scalar
        rd = [in0, in1] + ([scalar] if isinstance(scalar, SV) or hasattr(scalar, "tensor") else [])

        def fn(h):
            return h.scalar_tensor_tensor(o, x, sc, y, op0, op1)
        return self._record(eng, fn, rd, [out], tag=tag)

    def act(self, out, in_, func, bias=0.0, scale=1.0, tag=""):
        o, i = _ap(out), _ap(in_)
        bb = _ap(bias) if isinstance(bias, SV) else bias
        ss = _ap(scale) if isinstance(scale, SV) else scale
        rd = [in_] + [z for z in (bias, scale) if isinstance(z, SV)]

        def fn(h):
            return h.activation(o, i, func, bias=bb, scale=ss)
        return self._record("act", fn, rd, [out], tag=tag)

    def copy(self, eng, out, in_, tag=""):
        o, i = _ap(out), _ap(in_)
        if eng == "act":
            def fn(h):
                return h.copy(o, i)
        else:
            def fn(h):
                return h.tensor_copy(o, i)
        return self._record(eng, fn, [in_], [out], tag=tag)

    def red(self, out, in_, op, tag=""):
        o, i = _ap(out), _ap(in_)

        def fn(h):
            if op == "max":
                return h.reduce_max(o, i, axis=AXX)
            return h.reduce_sum(o, i, axis=AXX)
        return self._record("dve", fn, [in_], [out], tag=tag, sync_out=True)

    def recip(self, out, in_, tag=""):
        o, i = _ap(out), _ap(in_)

        def fn(h):
            return h.reciprocal(o, i)
        return self._record("dve", fn, [in_], [out], tag=tag, sync_out=True)

    def cumsum(self, out, in_, tag=""):
        o, i = _ap(out), _ap(in_)

        def fn(h):
            return h.tensor_tensor_scan(o, i, i, 0.0, ALU.add, ALU.bypass)
        return self._record("dve", fn, [in_], [out], tag=tag, sync_out=True)

    def memset(self, eng, ap, v, tag=""):
        o = _ap(ap)

        def fn(h):
            return h.memset(o, v)
        return self._record(eng, fn, [], [ap], tag=tag)

    def dma_gather(self, out, in_, idx, tag=""):
        sub = self._dma_sub("dg")
        o, i, ix = _ap(out), _ap(in_), _ap(idx)

        def fn(h):
            return h.indirect_dma_start(o, None, i, bass.IndirectOffsetOnAxis(ix, 1))
        r = self._record(sub, fn, [in_, idx], [out], iss="pool", tag=tag)
        if r.seq > 1:
            d = dict(r.waits)
            if d.get(sub, 0) < r.seq - 1:
                d[sub] = r.seq - 1
            r.waits = sorted(d.items())
        return r

    def materialize(self, block, sems):
        byiss = {}
        for r in self.recs:
            byiss.setdefault(r.iss, []).append(r)
        for e, lst in byiss.items():
            water = {}
            for r in lst:
                kept = []
                for (e2, s2) in r.waits:
                    if water.get(e2, -1) >= s2:
                        continue
                    kept.append((e2, s2))
                    water[e2] = s2
                r.waits = kept
        targets = {}
        for r in self.recs:
            for (e2, s2) in r.waits:
                targets.setdefault(e2, set()).add(s2)
        incd = {e: sorted(targets.get(e, ())) for e in ENGS}
        for r in self.recs:
            if r.eng in ENGS:
                r.inc = r.seq in targets.get(r.eng, ())
            else:
                r.inc = True

        def wval(e, s):
            if e not in ENGS:
                return 16 * s
            arr = incd[e]
            i = bisect.bisect_left(arr, s)
            assert i < len(arr), (e, s)
            return i + 1

        streams = {e: [] for e in ENGS}
        for r in self.recs:
            streams[r.iss].append(r)

        handles = {"pe": block.tensor, "dve": block.vector, "act": block.scalar,
                   "pool": block.gpsimd, "sp": block.sync}

        final_d = {s: 16 * self.counts[s] for s in self.dsubs}

        for ename in ENGS:
            deco = handles[ename]

            def body(h, _ename=ename):
                for r in streams[_ename]:
                    for (e2, s2) in r.waits:
                        h.wait_ge(sems[e2], wval(e2, s2))
                    try:
                        ins = r.fn(h)
                    except Exception as ex:
                        raise RuntimeError(
                            f"emit failed eng={r.eng} seq={r.seq} tag={r.tag!r} "
                            f"reads={r.reads} writes={r.writes}: {ex}") from ex
                    if ins is None:
                        continue
                    if r.eng not in ENGS:
                        ins.then_inc(sems[r.eng], 16)
                    elif r.inc:
                        ins.then_inc(sems[r.eng], 1)
                for s, v in final_d.items():
                    if v == 0:
                        continue
                    if (_ename == "sp" and s.startswith("ds")) or \
                       (_ename == "pool" and s.startswith("dg")):
                        h.wait_ge(sems[s], v)
            deco(body)



def build(dbg=(), upto=99, ln_stage=3):
    nc = bass.Bass("TRN2", target_bir_lowering=False)
    dt = nc.dram_tensor

    def inp(name, shape, dtype=F32):
        return dt(name, list(shape), dtype, kind="ExternalInput").ap()

    x0f = inp("x0f", [D, T])
    wq = inp("l0_wq", [D, D]); wk = inp("l0_wk", [D, D])
    wv = inp("l0_wv", [D, D]); ww = inp("l0_ww", [D, D])
    cq = inp("l0_cq", [D, 4]); ck = inp("l0_ck", [D, 4]); cv = inp("l0_cv", [D, 4])
    wf1 = inp("l0_wf1", [D, 128]); wf2 = inp("l0_wf2", [128, D])
    wb = inp("l0_wb", [D, 16]); wg1 = inp("l0_wg1", [D, 128]); wg2 = inp("l0_wg2", [128, D])
    bg2c = inp("l0_bg2", [128, 16]); nw0 = inp("l0_normw", [128, 1])
    wo0 = inp("l0_wo", [D, D])
    w0t = inp("l0_w0t", [D, 8192]); b0c = inp("l0_b0", [128, 64])
    w1t = inp("l0_w1t", [8192, D]); b1c = inp("l0_b1", [128, 16])
    wqt1 = inp("l1_wqt", [D, D]); wkt1 = inp("l1_wkt", [D, D]); wvt1 = inp("l1_wvt", [D, D])
    fq1t = inp("l1_fq1t", [128, 128]); fq2t = inp("l1_fq2t", [128, 128])
    fk1t = inp("l1_fk1t", [128, 128]); fk2t = inp("l1_fk2t", [128, 128])
    fq1b = inp("l1_fq1b", [1, 128]); fq2b = inp("l1_fq2b", [1, 128])
    fk1b = inp("l1_fk1b", [1, 128]); fk2b = inp("l1_fk2b", [1, 128])
    nw1 = inp("l1_normw", [128, 1])
    wot1 = inp("l1_wot", [D, D])
    tg = inp("l1_tg", [1024, NC1], I32); tu = inp("l1_tu", [1024, NC1], I32)
    gsc = inp("l1_gsc", [64, NC1]); usc = inp("l1_usc", [64, NC1])
    td = inp("l1_td", [8192, D], I32); dscx = inp("l1_dscx", [8192, D])
    jidx = inp("jidx", [128, 1], I32)
    id_bf = inp("ident_bf", [128, 128], BF16)
    id_f = inp("ident_f32", [128, 128])
    mask128 = inp("mask128", [128, 128])
    mask64 = inp("mask64", [64, 64])
    ones1 = inp("ones1", [1, 128])
    onescol = inp("onescol", [128, 1])
    oh127 = inp("oh127", [128, 1])
    oh16 = inp("oh16", [16, 16])
    pexp8 = inp("pexp8", [8, 128])

    out = dt("out", [D, TH], F32, kind="ExternalOutput").ap()
    dbg_out = {}
    for name, shape, dty in (
        ("d_xn0", [D, T], BF16), ("d_o0", [D, T], BF16), ("d_s0", [D, T], F32),
        ("d_e0", [D, T], F32), ("d_x1", [D, T], F32), ("d_xn1", [D, T], BF16),
        ("d_o1", [D, T], BF16), ("d_s1", [D, T], F32), ("d_inter", [NC1, TH], BF16),
        ("d_down", [D, TH], F32), ("d_h0", [128, 8 * 2048], F32),
        ("d_c0", [128, 16 * 128], F32),
    ):
        if name in dbg:
            dbg_out[name] = dt(name, shape, dty, kind="ExternalOutput").ap()

    o_hbm = dt("o_hbm", [D, T], BF16, kind="Internal").ap()
    s0_hbm = dt("s0_hbm", [D, T], F32, kind="Internal").ap()
    e_hbm = dt("e_hbm", [D, T], F32, kind="Internal").ap()
    x1_hbm = dt("x1_hbm", [D, T], F32, kind="Internal").ap()
    xn1_hbm = dt("xn1_hbm", [D, T], BF16, kind="Internal").ap()
    s1_hbm = dt("s1_hbm", [D, T], F32, kind="Internal").ap()
    inter_hbm = dt("inter_hbm", [NC1, TH], BF16, kind="Internal").ap()
    down_hbm = dt("down_hbm", [D, TH], F32, kind="Internal").ap()

    with ExitStack() as st_:
        e_ = st_.enter_context
        ar_xn = e_(nc.sbuf_tensor("ar_xn", [128, 32768], BF16))
        ar_b = e_(nc.sbuf_tensor("ar_b", [128, 32768], BF16))
        ar_c = e_(nc.sbuf_tensor("ar_c", [128, 8192], F32))
        ar_d = e_(nc.sbuf_tensor("ar_d", [128, 14336], BF16))
        sm32 = e_(nc.sbuf_tensor("sm32", [128, 2048], F32))
        smb = e_(nc.sbuf_tensor("smb", [128, 2688], BF16))
        cst = e_(nc.sbuf_tensor("cst", [128, 720], F32))
        cstb = e_(nc.sbuf_tensor("cstb", [128, 512], BF16))
        pA = e_(nc.psum_tensor("pA", [128, 512], F32))
        pB = e_(nc.psum_tensor("pB", [128, 512], F32))
        pC = e_(nc.psum_tensor("pC", [128, 512], F32))
        pD = e_(nc.psum_tensor("pD", [128, 512], F32))
        pE = e_(nc.psum_tensor("pE", [128, 512], F32))
        pF = e_(nc.psum_tensor("pF", [128, 512], F32))
        pG = e_(nc.psum_tensor("pG", [128, 512], F32))
        pH = e_(nc.psum_tensor("pH", [128, 512], F32))
        sems = {}
        for s in ("pe", "dve", "act", "pool", "sp"):
            sems[s] = e_(nc.semaphore("sem_" + s))
        for i_ in range(Prog.NDMA):
            sems["dg%d" % i_] = e_(nc.semaphore("sem_dg%d" % i_))
            sems["ds%d" % i_] = e_(nc.semaphore("sem_ds%d" % i_))
        block = e_(nc.Block())

        p = Prog(nc)

        def V(tens, c0, c1, key):
            return SV(tens.ap()[:, c0:c1], key)

        d32 = ar_d.ap().bitcast(F32)
        smbf = smb.ap().bitcast(F32)
        smbi = smb.ap().bitcast(I16)

        def xnv(dc, a, b_):
            return SV(ar_xn.ap()[:, dc * 2048 + a: dc * 2048 + b_], "xn_%d" % dc)

        c_idbf = SV(cstb.ap()[:, 0:128], "c_idbf")
        c_m128b = SV(cstb.ap()[:, 128:256], "c_m128b")
        c_idf = SV(cst.ap()[:, 0:128], "c_idf")
        c_m128 = SV(cst.ap()[:, 128:256], "c_m128")
        c_m64 = SV(cst.ap()[0:64, 256:320], "c_m64")
        c_ones1 = SV(cst.ap()[0:1, 320:448], "c_ones1")
        c_onescol = SV(cst.ap()[:, 448:449], "c_onescol")
        c_oh127 = SV(cst.ap()[:, 449:450], "c_oh127")
        p.dma("ds", c_idf, id_f)
        p.dma("ds", c_m128, mask128)
        p.dma("ds", c_m64, mask64)
        p.dma("ds", c_ones1, ones1)
        p.dma("ds", c_onescol, onescol)
        p.dma("ds", c_oh127, oh127)
        p.dma("dg", c_idbf, id_bf)
        p.dma("dg", c_m128b, mask128)
        cqs = SV(cst.ap()[:, 450:454], "c_cq")
        cks = SV(cst.ap()[:, 454:458], "c_ck")
        cvs = SV(cst.ap()[:, 458:462], "c_cv")
        c_bg2 = SV(cst.ap()[:, 462:478], "c_bg2")
        c_b0 = SV(cst.ap()[:, 478:542], "c_b0")
        c_b1 = SV(cst.ap()[:, 542:558], "c_b1")
        c_nw0 = SV(cst.ap()[:, 558:559], "c_nw0")
        c_nw1 = SV(cst.ap()[:, 559:560], "c_nw1")
        c_pexp8 = SV(cst.ap()[0:8, 564:692], "c_pexp8")
        c_jidx = SV(cst.ap()[:, 692:693].bitcast(I32), "c_jidx")
        c_eps5 = SV(cst.ap()[:, 712:713], "c_eps5")
        c_eps6 = SV(cst.ap()[:, 713:714], "c_eps6")
        c_one = SV(cst.ap()[:, 714:715], "c_one")
        idxw = SV(cst.ap()[:, 715:716].bitcast(I32), "idxw")
        p.dma("ds", c_bg2, bg2c)
        p.dma("ds", c_b0, b0c)
        p.dma("ds", c_b1, b1c)
        p.dma("ds", c_nw0, nw0)
        p.dma("ds", c_nw1, nw1)
        p.dma("ds", c_pexp8, pexp8)
        p.dma("ds", c_jidx, jidx)
        p.memset("dve", c_eps5, 1e-5)
        p.memset("dve", c_eps6, 1e-6)
        p.memset("dve", c_one, 1.0)
        c_fq1t = SV(cstb.ap()[:, 256:384], "c_fq1t")
        c_fq2t = SV(cstb.ap()[:, 384:512], "c_fq2t")
        cb2 = e_(nc.sbuf_tensor("cstb2", [128, 512], BF16))
        c_fk1t = SV(cb2.ap()[:, 0:128], "c_fk1t")
        c_fk2t = SV(cb2.ap()[:, 128:256], "c_fk2t")
        sv_wf2 = SV(ar_b.ap()[:, 28672:30720], "c_wf2")
        sv_wg2 = SV(ar_b.ap()[:, 30720:32768], "c_wg2")
        c_oh16 = SV(cst.ap()[0:16, 696:712], "c_oh16")
        p.dma("dg", c_fq1t, fq1t)
        p.dma("dg", c_fq2t, fq2t)
        p.dma("dg", c_fk1t, fk1t)
        p.dma("dg", c_fk2t, fk2t)
        p.dma("ds", c_oh16, oh16)

        if upto <= -1:
            p.materialize(block, sems)
            return nc, sorted(dbg_out)

        TT = (0, 512, 1024, 1536)

        def ln_pass(src_hbm, xn_writer, tag):
            xc = SV(sm32.ap()[:, 0:2048], "ln_xc")
            sq = SV(smbf[:, 0:512], "ln_sq")
            mrow = [SV(pX.ap()[0:1, 0:512], "p%s" % n)
                    for pX, n in ((pA, "A"), (pB, "B"), (pC, "C"), (pD, "D"))]
            vrow = [SV(pE.ap()[0:1, 0:512], "pE"), SV(pF.ap()[0:1, 0:512], "pF"),
                    SV(pG.ap()[0:1, 0:512], "pG"),
                    SV(pH.ap()[0:1, 0:512], "pH")]
            for dc in range(16):
                p.dma("ds", xc, SV(src_hbm[dc * 128:dc * 128 + 128, :], "hbm_src"), tag=tag)
                for ti in range(4):
                    xt = xc[:, TT[ti]:TT[ti] + 512]
                    p.mm(mrow[ti], c_onescol, xt, start=(dc == 0), stop=(dc == 15))
                    p.act(sq, xt, AF.Square)
                    p.mm(vrow[ti], c_onescol, sq, start=(dc == 0), stop=(dc == 15))
            if ln_stage <= 1:
                return
            rowm = SV(ar_c.ap()[0:1, 5120:5632], "rowm")
            rowv = SV(ar_c.ap()[0:1, 4096:4608], "rowv")
            rowr = SV(ar_c.ap()[0:1, 4608:5120], "rowr")
            mB = SV(ar_c.ap()[:, 0:2048], "ln_mB")
            rB = SV(ar_c.ap()[:, 2048:4096], "ln_rB")
            for ti in range(4):
                p.ts("dve", rowm, mrow[ti], 1.0 / D)
                p.ts("dve", rowv, vrow[ti], 1.0 / D)
                p.act(rowr, rowm, AF.Square)
                p.tt("dve", rowv, rowv, rowr, ALU.subtract)
                p.act(rowr, rowv, AF.Sqrt, bias=c_eps5[0:1, :])
                p.recip(rowr, rowr)
                p.mm(SV(pA.ap()[:, 0:512], "pA"), c_ones1, rowm)
                p.mm(SV(pE.ap()[:, 0:512], "pE"), c_ones1, rowr)
                p.copy("dve", mB[:, TT[ti]:TT[ti] + 512], SV(pA.ap()[:, 0:512], "pA"))
                p.copy("act", rB[:, TT[ti]:TT[ti] + 512], SV(pE.ap()[:, 0:512], "pE"))
            self_barrier = p.barrier()
            mB = SV(ar_c.ap()[:, 0:2048], "ln_mB")
            rB = SV(ar_c.ap()[:, 2048:4096], "ln_rB")
            if ln_stage <= 2:
                return
            tmpf = SV(sm32.ap()[:, 0:2048], "ln_xc")
            for dc in range(16):
                p.dma("ds", tmpf, SV(src_hbm[dc * 128:dc * 128 + 128, :], "hbm_src"), tag=tag)
                sub = SV(sm32.ap()[:, 0:2048], "ln_xc")
                p.tt("dve", sub, tmpf, mB, ALU.subtract)
                xn_writer(dc, sub, rB)

        def hbm_rows(hbm, r0, r1, key=None):
            return SV(hbm[r0:r1, :], key or _k(hbm))

        def load_wtiles(queue, dst_sv, hbm_slice, key_src, ccols=128):
            src = hbm_slice.rearrange("(c p) k -> p c k", p=128)
            dstv = SV(dst_sv.ap.rearrange("p (c k) -> p c k", k=ccols), dst_sv.key)
            p.dma(queue, dstv, SV(src, key_src))

        def w_xn0(dc, sub, rB):
            p.tt("dve", xnv(dc, 0, 2048), sub, rB, ALU.mult)
            if "d_xn0" in dbg_out:
                p.dma("ds", SV(dbg_out["d_xn0"][dc * 128:dc * 128 + 128, :], "d_xn0"),
                      xnv(dc, 0, 2048))
        ln_pass(x0f, w_xn0, "ln0")
        p.barrier()

        if upto <= 0:
            p.materialize(block, sems)
            return nc, sorted(dbg_out)

        p.dma("dg", sv_wf2, SV(wf2, "l0_wf2"))
        p.dma("dg", sv_wg2, SV(wg2, "l0_wg2"))
        wsl1 = SV(ar_b.ap()[:, 16384:18432], "wsl1")
        z1 = SV(ar_b.ap()[:, 24576:26624], "z1")
        g1t = SV(ar_b.ap()[:, 26624:28672], "g1t")
        beta_sb = SV(d32[0:16, 5120:7168], "beta_sb")
        for mat, matk, dst in ((wf1, "l0_wf1", z1), (wg1, "l0_wg1", g1t)):
            load_wtiles("dg", wsl1, mat, matk)
            for ti in range(4):
                pa = (SV(pA.ap()[:, 0:512], "pA") if ti % 2 == 0
                      else SV(pB.ap()[:, 0:512], "pB"))
                for dc in range(16):
                    p.mm(pa, wsl1[:, dc * 128:dc * 128 + 128],
                         xnv(dc, TT[ti], TT[ti] + 512), start=(dc == 0), stop=(dc == 15))
                p.copy("act", dst[:, TT[ti]:TT[ti] + 512], pa)
        wbsl = SV(ar_b.ap()[:, 16384:16640], "wsl1")
        load_wtiles("dg", wbsl, wb, "l0_wb", ccols=16)
        for ti in range(4):
            pa = (SV(pA.ap()[0:16, 0:512], "pA") if ti % 2 == 0
                  else SV(pB.ap()[0:16, 0:512], "pB"))
            for dc in range(16):
                p.mm(pa, wbsl[:, dc * 16:dc * 16 + 16],
                     xnv(dc, TT[ti], TT[ti] + 512), start=(dc == 0), stop=(dc == 15))
            p.act(beta_sb[:, TT[ti]:TT[ti] + 512], pa, AF.Sigmoid)
        p.barrier()

        if upto <= 1:
            p.materialize(block, sems)
            return nc, sorted(dbg_out)
        beta_sb = SV(d32[0:16, 5120:7168], "beta_sb")

        hq = SV(ar_b.ap()[:, 0:2048], "hq")
        hk = SV(ar_b.ap()[:, 2048:4096], "hk")
        hv = SV(ar_b.ap()[:, 4096:6144], "hv")
        hw = SV(ar_b.ap()[:, 6144:8192], "hw")
        hU = SV(ar_b.ap()[:, 8192:10240], "hU")
        hkt = SV(ar_b.ap()[:, 10240:12288], "hkt")
        hvt = SV(ar_b.ap()[:, 12288:14336], "hvt")
        hUt = SV(ar_b.ap()[:, 14336:16384], "hUt")
        hwq = SV(ar_b.ap()[:, 16384:18432], "hwq")
        hwk = SV(ar_b.ap()[:, 18432:20480], "hwk")
        hwv = SV(ar_b.ap()[:, 20480:22528], "hwv")
        hww = SV(ar_b.ap()[:, 22528:24576], "hww")
        cG = SV(ar_c.ap()[:, 0:2048], "cG")
        cGc = SV(ar_c.ap()[:, 2048:4096], "cGc")
        cgbw = SV(ar_c.ap()[:, 4096:6144], "cgbw")
        cgate = SV(ar_c.ap()[:, 6144:8192], "cgate")
        f1 = SV(d32[:, 0:2048], "f1")
        f6 = SV(d32[:, 2048:4096], "f6")
        o_h = SV(ar_d.ap()[:, 8192:10240], "o_h")
        st_et = SV(sm32.ap()[:, 0:128], "st_et")
        st_sk = SV(sm32.ap()[:, 128:256], "st_sk")
        st_sv = SV(sm32.ap()[:, 256:384], "st_sv")
        st_tmp = SV(sm32.ap()[:, 384:512], "st_tmp")
        st_tmp2 = SV(sm32.ap()[:, 512:640], "st_tmp2")
        dcol = SV(sm32.ap()[:, 640:656], "dcol")
        egm = SV(sm32.ap()[:, 656:672], "egm")
        colmx = SV(sm32.ap()[:, 672:673], "colmx")
        colmx2 = SV(sm32.ap()[:, 673:674], "colmx2")
        colsm = SV(sm32.ap()[:, 674:675], "colsm")
        colrc = SV(sm32.ap()[:, 675:676], "colrc")
        st_sk0s = SV(smb.ap()[:, 0:128], "st_sk0s")
        st_sv0s = SV(smb.ap()[:, 128:256], "st_sv0s")
        st_w2t = SV(smb.ap()[:, 256:384], "st_w2t")
        st_atm = SV(smb.ap()[:, 384:512], "st_atm")
        st_at2 = SV(smb.ap()[:, 512:640], "st_at2")
        stg = SV(smb.ap()[:, 640:1152], "stg")
        row_a = SV(smbf[0:1, 576:1088], "row_a")
        row_a128 = SV(smbf[0:1, 1088:1216], "row_a128")
        row_b128 = SV(smbf[0:1, 1216:1344], "row_b128")

        def proj4(wts, dsts):
            bks = [("pA", pA), ("pB", pB), ("pE", pE), ("pF", pF)]
            for ti in range(4):
                for mi in range(4):
                    kname, bank = bks[mi]
                    pa = SV(bank.ap()[:, 0:512], kname)
                    for dc in range(16):
                        p.mm(pa, wts[mi][:, dc * 128:dc * 128 + 128],
                             xnv(dc, TT[ti], TT[ti] + 512), start=(dc == 0), stop=(dc == 15))
                for mi in range(4):
                    kname, bank = bks[mi]
                    pa = SV(bank.ap()[:, 0:512], kname)
                    p.copy("act" if mi % 2 == 0 else "dve",
                           dsts[mi][:, TT[ti]:TT[ti] + 512], pa)

        def proj_conv(wt, cwv, out_bf, out_f32):
            for ti in range(4):
                pa = (SV(pA.ap()[:, 0:512], "pA") if ti % 2 == 0
                      else SV(pB.ap()[:, 0:512], "pB"))
                for dc in range(16):
                    p.mm(pa, wt[:, dc * 128:dc * 128 + 128],
                         xnv(dc, TT[ti], TT[ti] + 512), start=(dc == 0), stop=(dc == 15))
                p.copy("act" if ti % 2 == 0 else "dve", f1[:, TT[ti]:TT[ti] + 512], pa)
            p.ts("dve", f6, f1, cwv[:, 3:4])
            p.stt("dve", f6[:, 1:2048], f1[:, 0:2047], cwv[:, 2:3], f6[:, 1:2048], ALU.mult, ALU.add)
            p.stt("dve", f6[:, 2:2048], f1[:, 0:2046], cwv[:, 1:2], f6[:, 2:2048], ALU.mult, ALU.add)
            p.stt("dve", f6[:, 3:2048], f1[:, 0:2045], cwv[:, 0:1], f6[:, 3:2048], ALU.mult, ALU.add)
            p.act(out_f32 if out_f32 is not None else out_bf, f6, AF.Silu)

        for h in range(H):
            cs = 128 * h
            p.dma("ds", cqs, SV(cq[cs:cs + 128, :], "c_cq"))
            p.dma("ds", cks, SV(ck[cs:cs + 128, :], "c_ck"))
            p.dma("ds", cvs, SV(cv[cs:cs + 128, :], "c_cv"))
            for mat, matk, slot in ((wq, "l0_wq", hwq), (wk, "l0_wk", hwk),
                                    (wv, "l0_wv", hwv), (ww, "l0_ww", hww)):
                load_wtiles("dg", slot, mat[:, cs:cs + 128], matk)
            proj4((hwq, hwk, hwv, hww), (hU, hUt, hkt, hvt))

            def conv_silu2(rawb, cwv, out_bf, out_f32, eng, scr):
                p.ts(eng, scr, rawb, cwv[:, 3:4])
                p.stt(eng, scr[:, 1:2048], rawb[:, 0:2047], cwv[:, 2:3], scr[:, 1:2048], ALU.mult, ALU.add)
                p.stt(eng, scr[:, 2:2048], rawb[:, 0:2046], cwv[:, 1:2], scr[:, 2:2048], ALU.mult, ALU.add)
                p.stt(eng, scr[:, 3:2048], rawb[:, 0:2045], cwv[:, 0:1], scr[:, 3:2048], ALU.mult, ALU.add)
                p.act(out_f32 if out_f32 is not None else out_bf, scr, AF.Silu)
            f6b = SV(ar_c.ap()[:, 6144:8192], "cgate")
            conv_silu2(hU, cqs, hq, None, "dve", f6)
            conv_silu2(hUt, cks, hk, None, "dve", f6b)
            conv_silu2(hkt, cvs, hv, None, "dve", f6)
            conv_silu2(hvt, cvs, None, f1, "dve", f6b)
            for ti in range(4):
                pa = (SV(pA.ap()[:, 0:512], "pA") if ti % 2 == 0
                      else SV(pB.ap()[:, 0:512], "pB"))
                p.mm(pa, SV(sv_wf2.ap[:, cs:cs + 128], "c_wf2"), z1[:, TT[ti]:TT[ti] + 512])
                p.copy("act" if ti % 2 == 0 else "dve", cgbw[:, TT[ti]:TT[ti] + 512], pa)
            p.act(f6, cgbw, AF.Abs)
            p.act(cgate, f6, AF.Exp, scale=-1.0)
            p.act(f6, cgate, AF.Ln, bias=c_one)
            p.ts("dve", cgbw, cgbw, 0.0, op0=ALU.min)
            p.tt("dve", cgbw, cgbw, f6, ALU.subtract)
            for ch in range(16):
                a = ch * 128
                p.cumsum(cG[:, a:a + 128], cgbw[:, a:a + 128])
            p.act(dcol, SV(cG.ap[:, 127:2048:128], "cG"), AF.Exp)
            p.act(egm, SV(cG.ap[:, 63:2048:128], "cG"), AF.Exp)
            for ch in range(16):
                a = ch * 128
                p.ts("dve", cGc[:, a:a + 128], cG[:, a:a + 128],
                     SV(cG.ap[:, a + 63:a + 64], "cG"), op0=ALU.subtract)
            p.act(cG, cGc, AF.Exp, scale=-1.0)
            for ti in range(4):
                sl0, sl1 = TT[ti], TT[ti] + 512
                p.act(f6[:, sl0:sl1], f1[:, sl0:sl1], AF.Square)
                pc1 = SV(pC.ap()[0:1, 0:512], "pC")
                p.mm(pc1, c_onescol, f6[:, sl0:sl1])
                p.act(row_a, pc1, AF.Sqrt, bias=c_eps6[0:1, :])
                p.recip(row_a, row_a)
                pd1 = SV(pD.ap()[:, 0:512], "pD")
                p.mm(pd1, c_ones1, row_a)
                p.tt("dve", f6[:, sl0:sl1], f1[:, sl0:sl1], pd1, ALU.mult)
                pcr = SV(pC.ap()[0:1, 0:512], "pC")
                p.mm(pcr, SV(c_oh16.ap[:, h:h + 1], "c_oh16"),
                     SV(beta_sb.ap[:, sl0:sl1], "beta_sb"))
                p.copy("dve", row_a, pcr)
                pc2 = SV(pC.ap()[:, 0:512], "pC")
                p.mm(pc2, c_ones1, row_a)
                p.tt("dve", cgbw[:, sl0:sl1], f6[:, sl0:sl1], pc2, ALU.mult)
            p.tt("dve", hU, cgbw, cG, ALU.mult)
            for ch in range(16):
                a = ch * 128
                pg = SV(pG.ap().bitcast(BF16)[:, 0:128], "pG")
                p.tr(pg, hU[:, a:a + 128], c_idbf)
                p.copy("dve", hUt[:, a:a + 128], pg)
                ph_ = SV(pH.ap().bitcast(BF16)[:, 0:128], "pH")
                p.tr(ph_, hk[:, a:a + 128], c_idbf)
                p.copy("act", hkt[:, a:a + 128], ph_)
                pg2 = SV(pG.ap().bitcast(BF16)[:, 128:256], "pG")
                p.tr(pg2, hv[:, a:a + 128], c_idbf)
                p.copy("dve", hvt[:, a:a + 128], pg2)
            et_all = SV(ar_c.ap()[:, 6144:8192], "cgate")
            at_all = SV(sm32.ap().bitcast(BF16)[:, 1536:3584], "at_all")
            w2_all = SV(ar_d.ap()[:, 4096:8192], "f6")
            egmr = SV(ar_c.ap()[0:1, 4096:6144], "cgbw")
            dfr = SV(ar_c.ap()[0:1, 0:2048], "cG")
            for ch in range(16):
                a = ch * 128
                pd1 = (SV(pD.ap()[:, 0:128], "pD") if ch % 2 == 0
                       else SV(pD.ap()[:, 128:256], "pD"))
                p.tr(pd1, cGc[:, a:a + 128], c_idf)
                p.act(et_all[:, a:a + 128], pd1, AF.Exp)
            for ch in range(16):
                a = ch * 128
                pa1 = (SV(pA.ap()[:, 0:128], "pA") if ch % 2 == 0
                       else SV(pA.ap()[:, 128:256], "pA"))
                p.mm(pa1, hk[:, a:a + 128], hq[:, a:a + 128])
                p.tt("dve", at_all[:, a:a + 128], pa1, c_m128b, ALU.mult)
            for ch in range(16):
                a = ch * 128
                pc1 = SV(pC.ap()[0:1, 0:128], "pC")
                p.mm(pc1, c_oh127, et_all[:, a:a + 128])
                p.copy("dve", row_a128, pc1)
                pcB = SV(pC.ap()[:, 128:256], "pC")
                p.mm(pcB, c_ones1, row_a128)
                p.tt("dve", w2_all[:, a:a + 128], hUt[:, a:a + 128], pcB, ALU.mult)
                pdR = SV(pD.ap()[0:1, 256:384], "pD")
                p.tr(pdR, egm[:, ch:ch + 1], c_idf)
                p.copy("act", SV(egmr.ap[:, a:a + 128], "cgbw"), pdR)
                pdR2 = SV(pD.ap()[0:1, 384:512], "pD")
                p.tr(pdR2, dcol[:, ch:ch + 1], c_idf)
                p.copy("dve", SV(dfr.ap[:, a:a + 128], "cG"), pdR2)
            p.memset("dve", st_sk, 0.0)
            p.memset("dve", st_sv, 0.0)
            for ch in range(16):
                a = ch * 128
                pa2 = SV(pA.ap()[:, 128:256], "pA")
                pa3 = SV(pA.ap()[:, 256:384], "pA")
                pa4 = SV(pA.ap()[:, 384:512], "pA")
                pb1 = SV(pB.ap()[:, 0:128], "pB")
                pb2 = SV(pB.ap()[:, 128:256], "pB")
                pcE = SV(pC.ap()[:, 256:384], "pC")
                pcD = SV(pC.ap()[:, 384:512], "pC")
                ph1_ = SV(pH.ap().bitcast(BF16)[:, 0:128], "pH")
                p.mm(pcE, c_ones1, SV(egmr.ap[:, a:a + 128], "cgbw"))
                p.tt("dve", st_sk0s, st_sk, pcE, ALU.mult)
                p.ts("dve", st_sv0s, st_sv, egm[:, ch:ch + 1], op0=ALU.mult)
                p.mm(pa2, hq[:, a:a + 128], st_sk0s, start=True, stop=False)
                p.mm(pa2, SV(at_all.ap[:, a:a + 128], "at_all"), hUt[:, a:a + 128],
                     start=False, stop=True)
                p.tt("dve", st_tmp, pa2, et_all[:, a:a + 128], ALU.mult)
                p.red(colmx, st_tmp, "max")
                p.ts("dve", colmx2, colmx, -SCALE)
                p.act(st_tmp2, st_tmp, AF.Exp, bias=colmx2, scale=SCALE)
                p.red(colsm, st_tmp2, "sum")
                p.recip(colrc, colsm)
                p.stt("dve", st_atm, st_tmp2, colrc, et_all[:, a:a + 128], ALU.mult, ALU.mult)
                p.tr(ph1_, st_atm, c_idbf)
                p.copy("dve", st_at2, ph1_)
                p.mm(pb1, st_sv0s, st_at2, start=True, stop=False)
                p.mm(pa3, hU[:, a:a + 128], st_at2)
                p.tt("dve", st_sk0s, pa3, c_m128b, ALU.mult)
                p.mm(pb1, hvt[:, a:a + 128], st_sk0s, start=False, stop=True)
                p.copy("act", o_h[:, a:a + 128], pb1)
                p.mm(pa4, hkt[:, a:a + 128], SV(w2_all.ap[:, a:a + 128], "f6"))
                p.mm(pb2, SV(w2_all.ap[:, a:a + 128], "f6"), hvt[:, a:a + 128])
                p.stt("dve", st_sv, st_sv, dcol[:, ch:ch + 1], pb2, ALU.mult, ALU.add)
                p.mm(pcD, c_ones1, SV(dfr.ap[:, a:a + 128], "cG"))
                p.tt("dve", st_tmp, st_sk, pcD, ALU.mult)
                p.tt("dve", st_sk, st_tmp, pa4, ALU.add)
            for ti in range(4):
                pa = (SV(pA.ap()[:, 0:512], "pA") if ti % 2 == 0
                      else SV(pB.ap()[:, 0:512], "pB"))
                p.mm(pa, SV(sv_wg2.ap[:, cs:cs + 128], "c_wg2"), g1t[:, TT[ti]:TT[ti] + 512])
                p.act(SV(ar_c.ap()[:, 6144 + TT[ti]:6144 + TT[ti] + 512], "cgate"), pa,
                      AF.Sigmoid, bias=c_bg2[:, h:h + 1])
            for ti in range(4):
                sl0, sl1 = TT[ti], TT[ti] + 512
                p.act(f6[:, sl0:sl1], o_h[:, sl0:sl1], AF.Square)
                pc1 = SV(pC.ap()[0:1, 0:512], "pC")
                p.mm(pc1, c_onescol, f6[:, sl0:sl1])
                p.act(row_a, pc1, AF.Sqrt, bias=c_eps5[0:1, :], scale=1.0 / 128)
                p.recip(row_a, row_a)
                pd1 = SV(pD.ap()[:, 0:512], "pD")
                p.mm(pd1, c_ones1, row_a)
                p.stt("dve", f6[:, sl0:sl1], o_h[:, sl0:sl1], c_nw0, pd1, ALU.mult, ALU.mult)
                p.tt("dve", stg, f6[:, sl0:sl1], cgate[:, sl0:sl1], ALU.mult)
                p.dma("ds", SV(o_hbm[cs:cs + 128, sl0:sl1], "o_hbm"), stg)
        if "d_o0" in dbg_out:
            p.dma("ds", SV(dbg_out["d_o0"], "d_o0"), SV(o_hbm, "o_hbm"))
        p.barrier()

        if upto <= 2:
            p.materialize(block, sems)
            return nc, sorted(dbg_out)

        oh3 = SV(ar_b.ap()[:, 0:16384], "oh3")
        wt3 = SV(ar_b.ap()[:, 16384:18432], "wt3")
        stg3 = SV(sm32.ap()[:, 0:512], "stg3")
        for t2 in range(2):
            src = o_hbm.rearrange("(c p) t -> p c t", p=128)[:, :, t2 * 1024:(t2 + 1) * 1024]
            p.dma("ds", SV(oh3.ap.rearrange("p (c t) -> p c t", t=1024), "oh3"),
                  SV(src, "o_hbm"))
            for dc in range(16):
                wt3 = SV(ar_b.ap()[:, 16384 + (dc % 2) * 2048: 16384 + (dc % 2) * 2048 + 2048],
                         "wt3_%d" % (dc % 2))
                load_wtiles("dg", wt3, wo0[:, dc * 128:dc * 128 + 128], "l0_wo")
                for t5 in range(2):
                    pa = (SV(pA.ap()[:, 0:512], "pA") if (dc * 2 + t5) % 2 == 0
                          else SV(pB.ap()[:, 0:512], "pB"))
                    for hf in range(16):
                        p.mm(pa, wt3[:, hf * 128:hf * 128 + 128],
                             oh3[:, hf * 1024 + t5 * 512: hf * 1024 + t5 * 512 + 512],
                             start=(hf == 0), stop=(hf == 15))
                    stg3x = stg3 if t5 == 0 else SV(sm32.ap()[:, 512:1024], "stg3b")
                    p.copy("dve" if t5 == 0 else "act", stg3x, pa)
                    c0_ = t2 * 1024 + t5 * 512
                    p.dma("ds", SV(s0_hbm[dc * 128:dc * 128 + 128, c0_:c0_ + 512], "s0_hbm"), stg3x)
        if "d_s0" in dbg_out:
            p.dma("ds", SV(dbg_out["d_s0"], "d_s0"), SV(s0_hbm, "s0_hbm"))
        p.barrier()

        if upto <= 3:
            p.materialize(block, sems)
            return nc, sorted(dbg_out)

        d1 = SV(ar_b.ap()[:, 0:32768], "d1")
        for q4 in range(4):
            nq = 2048 * q4
            for nb in range(16):
                w0tl = SV(ar_d.ap()[:, (nb % 2) * 2048:(nb % 2) * 2048 + 2048], "w0tl%d" % (nb % 2))
                load_wtiles("dg", w0tl, w0t[:, nq + nb * 128: nq + nb * 128 + 128], "l0_w0t")
                bks4 = [("pA", pA), ("pB", pB), ("pE", pE), ("pF", pF)]
                for ti in range(4):
                    kname, bank = bks4[ti]
                    pa = SV(bank.ap()[:, 0:512], kname)
                    for dc in range(16):
                        p.mm(pa, w0tl[:, dc * 128:dc * 128 + 128],
                             xnv(dc, TT[ti], TT[ti] + 512), start=(dc == 0), stop=(dc == 15))
                    p.act(d1[:, nb * 2048 + TT[ti]: nb * 2048 + TT[ti] + 512], pa,
                          AF.Gelu, bias=c_b0[:, q4 * 16 + nb: q4 * 16 + nb + 1])
            for dc in range(16):
                w1tl = SV(ar_d.ap()[:, 4096 + (dc % 2) * 2048: 4096 + (dc % 2) * 2048 + 2048],
                          "w1tl%d" % (dc % 2))
                load_wtiles("dg", w1tl, w1t[nq:nq + 2048, dc * 128:dc * 128 + 128], "l0_w1t")
                bksE = [("pC", pC), ("pD", pD), ("pG", pG), ("pH", pH)]
                for ti in range(4):
                    kname, bank = bksE[ti]
                    pc_ = SV(bank.ap()[:, 0:512], kname)
                    for nb in range(16):
                        p.mm(pc_, w1tl[:, nb * 128:nb * 128 + 128],
                             d1[:, nb * 2048 + TT[ti]: nb * 2048 + TT[ti] + 512],
                             start=(nb == 0), stop=(nb == 15))
                    stgx = stg3 if ti % 2 == 0 else SV(sm32.ap()[:, 512:1024], "stg4b")
                    p.copy("dve" if ti % 2 == 0 else "act", stgx, pc_)
                    dst = SV(e_hbm[dc * 128:dc * 128 + 128, TT[ti]:TT[ti] + 512], "e_hbm")
                    if q4 == 0:
                        p.dma("ds", dst, stgx)
                    else:
                        p.dma("dg", dst, stgx, accum=True)
        if "d_e0" in dbg_out:
            p.dma("ds", SV(dbg_out["d_e0"], "d_e0"), SV(e_hbm, "e_hbm"))
        p.barrier()

        if upto <= 4:
            p.materialize(block, sems)
            return nc, sorted(dbg_out)

        ec5 = SV(sm32.ap()[:, 0:2048], "ec5")
        sq5 = SV(smbf[:, 0:512], "sq5")
        s0c5 = SV(ar_c.ap()[:, 0:2048], "s0c5")
        x0c5 = SV(ar_c.ap()[:, 2048:4096], "x0c5")
        g0c5 = SV(ar_c.ap()[:, 4096:6144], "g0c5")
        x1c5 = SV(ar_c.ap()[:, 6144:8192], "x1c5")
        mrow5 = [SV(pA.ap()[0:1, 0:512], "pA"), SV(pB.ap()[0:1, 0:512], "pB"),
                 SV(pC.ap()[0:1, 0:512], "pC"), SV(pD.ap()[0:1, 0:512], "pD")]
        vrow5 = [SV(pE.ap()[0:1, 0:512], "pE"), SV(pF.ap()[0:1, 0:512], "pF"),
                 SV(pG.ap()[0:1, 0:512], "pG"),
                 SV(pH.ap()[0:1, 0:512], "pH")]
        for dc in range(16):
            p.dma("ds", ec5, hbm_rows(e_hbm, dc * 128, dc * 128 + 128))
            p.dma("ds", s0c5, hbm_rows(s0_hbm, dc * 128, dc * 128 + 128))
            p.dma("ds", x0c5, hbm_rows(x0f, dc * 128, dc * 128 + 128, "x0f"))
            p.act(g0c5, ec5, AF.Silu, bias=c_b1[:, dc:dc + 1])
            p.tt("dve", g0c5, s0c5, g0c5, ALU.mult)
            p.tt("dve", x1c5, x0c5, g0c5, ALU.add)
            p.dma("ds", SV(x1_hbm[dc * 128:dc * 128 + 128, :], "x1_hbm"), x1c5)
            p.copy("dve", SV(ar_b.ap()[:, dc * 2048:dc * 2048 + 2048], "x1b_%d" % dc), x1c5)
            for ti in range(4):
                xt = x1c5[:, TT[ti]:TT[ti] + 512]
                p.mm(mrow5[ti], c_onescol, xt, start=(dc == 0), stop=(dc == 15))
                p.act(sq5, xt, AF.Square)
                p.mm(vrow5[ti], c_onescol, sq5, start=(dc == 0), stop=(dc == 15))
        if "d_x1" in dbg_out:
            p.dma("ds", SV(dbg_out["d_x1"], "d_x1"), SV(x1_hbm, "x1_hbm"))
        p.barrier()
        mrow5 = [SV(pA.ap()[0:1, 0:512], "pA"), SV(pB.ap()[0:1, 0:512], "pB"),
                 SV(pC.ap()[0:1, 0:512], "pC"), SV(pD.ap()[0:1, 0:512], "pD")]
        vrow5 = [SV(pE.ap()[0:1, 0:512], "pE"), SV(pF.ap()[0:1, 0:512], "pF"),
                 SV(pG.ap()[0:1, 0:512], "pG"),
                 SV(pH.ap()[0:1, 0:512], "pH")]
        rowm = SV(ar_c.ap()[0:1, 5120:5632], "rowm")
        rowv = SV(ar_c.ap()[0:1, 4096:4608], "rowv")
        rowr = SV(ar_c.ap()[0:1, 4608:5120], "rowr")
        for ti in range(4):
            p.ts("dve", rowm, mrow5[ti], 1.0 / D)
            p.ts("dve", rowv, vrow5[ti], 1.0 / D)
            p.act(rowr, rowm, AF.Square)
            p.tt("dve", rowv, rowv, rowr, ALU.subtract)
            p.act(rowr, rowv, AF.Sqrt, bias=c_eps5[0:1, :])
            p.recip(rowr, rowr)
            pa = SV(pA.ap()[:, 0:512], "pA")
            pb = SV(pE.ap()[:, 0:512], "pE")
            p.mm(pa, c_ones1, rowm)
            p.mm(pb, c_ones1, rowr)
            p.copy("dve", SV(ar_c.ap()[:, TT[ti]:TT[ti] + 512], "mB1"), pa)
            p.copy("act", SV(ar_c.ap()[:, 2048 + TT[ti]:2048 + TT[ti] + 512], "rB1"), pb)
        p.barrier()
        mB1 = SV(ar_c.ap()[:, 0:2048], "mB1")
        rB1 = SV(ar_c.ap()[:, 2048:4096], "rB1")
        sub5 = SV(ar_c.ap()[:, 4096:6144], "sub5")
        for dc in range(16):
            xb = SV(ar_b.ap()[:, dc * 2048:dc * 2048 + 2048], "x1b_%d" % dc)
            p.tt("dve", sub5, xb, mB1, ALU.subtract)
            p.tt("dve", xnv(dc, 0, 2048), sub5, rB1, ALU.mult)
            p.dma("ds", SV(xn1_hbm[dc * 128:dc * 128 + 128, :], "xn1_hbm"), xnv(dc, 0, 2048))
        if "d_xn1" in dbg_out:
            p.dma("ds", SV(dbg_out["d_xn1"], "d_xn1"), SV(xn1_hbm, "xn1_hbm"))
        p.barrier()

        if upto <= 5:
            p.materialize(block, sems)
            return nc, sorted(dbg_out)

        kp6 = SV(ar_b.ap()[:, 0:2048], "kp6")
        vp6 = SV(ar_b.ap()[:, 2048:4096], "vp6")
        kt6 = SV(ar_b.ap()[:, 4096:8192], "kt6")
        vt6 = SV(ar_b.ap()[:, 8192:12288], "vt6")
        qp6 = SV(ar_b.ap()[:, 12288:14336], "qp6")
        kraw = SV(ar_b.ap()[:, 14336:16384], "kraw")
        qraw = SV(ar_b.ap()[:, 18432:20480], "qraw")
        wsl6s = [SV(ar_b.ap()[:, 16384:18432], "wsl6a"),
                 SV(ar_b.ap()[:, 20480:22528], "wsl6b")]
        fm1 = SV(d32[:, 0:2048], "fm1")
        fm2 = SV(d32[:, 2048:4096], "fm2")
        o6 = SV(d32[:, 4096:6144], "o6")
        kv_sb = SV(smb.ap()[:, 0:128], "kv_sb")
        sm32b = sm32.ap().bitcast(BF16)
        ones512b = SV(sm32b[0:1, 0:512], "ones512b")
        b_fq1 = SV(sm32b[0:1, 512:640], "b_fq1")
        b_fq2 = SV(sm32b[0:1, 640:768], "b_fq2")
        b_fk1 = SV(sm32b[0:1, 768:896], "b_fk1")
        b_fk2 = SV(sm32b[0:1, 896:1024], "b_fk2")
        p.memset("dve", ones512b, 1.0)
        p.dma("dg", b_fq1, fq1b)
        p.dma("dg", b_fq2, fq2b)
        p.dma("dg", b_fk1, fk1b)
        p.dma("dg", b_fk2, fk2b)
        stm6 = SV(smb.ap()[0:64, 128:192], "stm6")
        stg6 = SV(smb.ap()[:, 640:1152], "stg")

        def proj6(matk, mat, cs, dst, nt, wi):
            wsl = wsl6s[wi % 2]
            load_wtiles("dg", wsl, mat[:, cs:cs + 128], matk)
            bks6 = [("pA", pA), ("pB", pB), ("pE", pE), ("pF", pF)]
            for ti in range(nt):
                kname, bank = bks6[ti % 4]
                pa = SV(bank.ap()[:, 0:512], kname)
                for dc in range(16):
                    p.mm(pa, wsl[:, dc * 128:dc * 128 + 128],
                         xnv(dc, TT[ti], TT[ti] + 512), start=(dc == 0), stop=(dc == 15))
                p.copy("act" if ti % 2 == 0 else "dve", dst[:, TT[ti]:TT[ti] + 512], pa)

        def fmap6(srcb, w1c, b1c_, w2c, b2c_, dst, nt, scaled):
            for ti in range(nt):
                sl0, sl1 = TT[ti], TT[ti] + 512
                pd_ = SV(pD.ap()[:, 0:512], "pD")
                p.mm(pd_, w1c, srcb[:, sl0:sl1], start=True, stop=False)
                p.mm(pd_, b1c_, ones512b, start=False, stop=True)
                p.copy("act", fm1[:, sl0:sl1], pd_)
                pe_ = SV(pF.ap()[:, 0:512], "pF")
                p.mm(pe_, w2c, srcb[:, sl0:sl1], start=True, stop=False)
                p.mm(pe_, b2c_, ones512b, start=False, stop=True)
                p.copy("dve", fm2[:, sl0:sl1], pe_)
                if scaled:
                    p.stt("dve", dst[:, sl0:sl1], fm1[:, sl0:sl1], SCALE,
                          fm2[:, sl0:sl1], ALU.mult, ALU.mult)
                else:
                    p.tt("dve", dst[:, sl0:sl1], fm1[:, sl0:sl1], fm2[:, sl0:sl1], ALU.mult)

        for h in range(H):
            cs = 128 * h
            proj6("l1_wqt", wqt1, cs, qraw, 4, 0)
            fmap6(qraw, c_fq1t, b_fq1, c_fq2t, b_fq2, qp6, 4, True)
            proj6("l1_wkt", wkt1, cs, kraw, 4, 1)
            fmap6(kraw, c_fk1t, b_fk1, c_fk2t, b_fk2, kp6, 4, False)
            proj6("l1_wvt", wvt1, cs, vp6, 4, 0)
            for tc in range(32):
                a = tc * 64
                pg = SV(pG.ap().bitcast(BF16)[0:64, 0:128], "pG")
                p.tr(pg, kp6[:, a:a + 64], c_idbf)
                p.copy("dve", SV(kt6.ap[0:64, tc * 128:tc * 128 + 128], "kt6"), pg)
                ph_ = SV(pH.ap().bitcast(BF16)[0:64, 0:128], "pH")
                p.tr(ph_, vp6[:, a:a + 64], c_idbf)
                p.copy("act", SV(vt6.ap[0:64, tc * 128:tc * 128 + 128], "vt6"), ph_)
            p.memset("dve", kv_sb, 0.0)
            for cc in range(32):
                a64 = cc * 64
                ktv = SV(kt6.ap[0:64, cc * 128:cc * 128 + 128], "kt6")
                vtv = SV(vt6.ap[0:64, cc * 128:cc * 128 + 128], "vt6")
                pa1 = SV(pA.ap()[:, 0:64], "pA")
                p.mm(pa1, kv_sb, qp6[:, a64:a64 + 64], start=True, stop=False)
                pb1 = SV(pB.ap()[0:64, 0:64], "pB")
                p.mm(pb1, kp6[:, a64:a64 + 64], qp6[:, a64:a64 + 64])
                p.tt("dve", stm6, pb1, c_m64, ALU.mult)
                p.mm(pa1, vtv, stm6, start=False, stop=True)
                p.copy("act", o6[:, a64:a64 + 64], pa1)
                pe1 = SV(pE.ap()[:, 0:128], "pE")
                p.mm(pe1, ktv, vtv, start=(cc == 0), stop=(cc == 31))
                if cc < 31:
                    p.copy("dve", kv_sb, pe1)
            for ti in range(4):
                sl0, sl1 = TT[ti], TT[ti] + 512
                p.act(fm1[:, sl0:sl1], o6[:, sl0:sl1], AF.Square)
                pc1 = SV(pC.ap()[0:1, 0:512], "pC")
                p.mm(pc1, c_onescol, fm1[:, sl0:sl1])
                p.act(row_a, pc1, AF.Sqrt, bias=c_eps5[0:1, :], scale=1.0 / 128)
                p.recip(row_a, row_a)
                pd1 = SV(pD.ap()[:, 0:512], "pD")
                p.mm(pd1, c_ones1, row_a)
                p.stt("dve", stg6, o6[:, sl0:sl1], c_nw1, pd1, ALU.mult, ALU.mult)
                p.dma("ds", SV(o_hbm[cs:cs + 128, sl0:sl1], "o1_hbm"), stg6)
        if "d_o1" in dbg_out:
            p.dma("ds", SV(dbg_out["d_o1"], "d_o1"), SV(o_hbm, "o1_hbm"))
        p.barrier()

        if upto <= 6:
            p.materialize(block, sems)
            return nc, sorted(dbg_out)

        oh7 = SV(ar_b.ap()[:, 0:16384], "oh7")
        wt7 = SV(ar_b.ap()[:, 16384:18432], "wt7")
        stg7 = SV(sm32.ap()[:, 0:512], "stg7")
        for t2 in range(2):
            src = o_hbm.rearrange("(c p) t -> p c t", p=128)[:, :, t2 * 1024:(t2 + 1) * 1024]
            p.dma("ds", SV(oh7.ap.rearrange("p (c t) -> p c t", t=1024), "oh7"),
                  SV(src, "o1_hbm"))
            for dc in range(16):
                wt7 = SV(ar_b.ap()[:, 16384 + (dc % 2) * 2048: 16384 + (dc % 2) * 2048 + 2048],
                         "wt7_%d" % (dc % 2))
                load_wtiles("dg", wt7, wot1[:, dc * 128:dc * 128 + 128], "l1_wot")
                for t5 in range(2):
                    pa = (SV(pA.ap()[:, 0:512], "pA") if (dc * 2 + t5) % 2 == 0
                          else SV(pB.ap()[:, 0:512], "pB"))
                    for hf in range(16):
                        p.mm(pa, wt7[:, hf * 128:hf * 128 + 128],
                             oh7[:, hf * 1024 + t5 * 512: hf * 1024 + t5 * 512 + 512],
                             start=(hf == 0), stop=(hf == 15))
                    stg7x = stg7 if t5 == 0 else SV(sm32.ap()[:, 512:1024], "stg7b")
                    p.copy("dve" if t5 == 0 else "act", stg7x, pa)
                    c0_ = t2 * 1024 + t5 * 512
                    p.dma("ds", SV(s1_hbm[dc * 128:dc * 128 + 128, c0_:c0_ + 512], "s1_hbm"), stg7x)
        if "d_s1" in dbg_out:
            p.dma("ds", SV(dbg_out["d_s1"], "d_s1"), SV(s1_hbm, "s1_hbm"))
        p.barrier()

        if upto <= 7:
            p.materialize(block, sems)
            return nc, sorted(dbg_out)

        xno = SV(ar_xn.ap()[:, 0:16384], "xno")
        for dc in range(16):
            p.ts("dve", idxw, c_jidx, dc * 128 * T, None, ALU.add)
            p.dma_gather(SV(ar_xn.ap()[:, dc * 1024:dc * 1024 + 1024], "xno"),
                         hbm_rows(xn1_hbm, 0, 128, "xn1_hbm"), idxw)
        interblk = SV(ar_b.ap()[:, 0:4096], "interblk")
        gdqs = [SV(ar_b.ap()[:, 8192:16384], "gdq0"), SV(ar_b.ap()[:, 16384:24576], "gdq1")]
        tgb = SV(smbi[:, 0:512], "tgb")
        lo16 = SV(smbi[:, 512:1024], "lo16")
        hi16 = SV(smbi[:, 1024:1536], "hi16")
        sg16 = SV(smbi[:, 1536:2048], "sg16")
        lo_bf = SV(smb.ap()[:, 512:1024], "lo16")
        hi_bf = SV(smb.ap()[:, 1024:1536], "hi16")
        scsl = SV(sm32.ap()[0:8, 0:512], "scsl")
        gblk = SV(ar_c.ap()[:, 0:4096], "gblk")
        g2a = SV(sm32.ap()[:, 1024:1536], "g2a")
        g2b = SV(sm32.ap()[:, 1536:2048], "g2b")

        def dequant8(packed, packedk, scales, scalesk, nb0, gdq):
            for qc in range(8):
                p.dma("dg", tgb,
                      SV(packed[qc * 128:qc * 128 + 128, nb0:nb0 + 512], packedk))
                p.ts("dve", lo16, tgb, 7, 6, ALU.bitwise_and, ALU.logical_shift_left)
                p.ts("dve", sg16, tgb, 8, 12, ALU.bitwise_and, ALU.logical_shift_left)
                p.tt("dve", lo16, lo16, sg16, ALU.bitwise_or)
                p.ts("dve", hi16, tgb, 0x70, 2, ALU.bitwise_and, ALU.logical_shift_left)
                p.ts("dve", sg16, tgb, 0x80, 8, ALU.bitwise_and, ALU.logical_shift_left)
                p.tt("dve", hi16, hi16, sg16, ALU.bitwise_or)
                p.dma("ds", scsl, SV(scales[qc * 8:qc * 8 + 8, nb0:nb0 + 512], scalesk))
                pcx = SV(pC.ap()[:, 0:512], "pC")
                p.mm(pcx, c_pexp8, scsl)
                scb = SV(smb.ap()[:, 2048:2560], "scb8")
                p.act(scb, pcx, AF.Copy)
                base = qc * 1024
                p.stt("dve", gdq[:, base:base + 512], lo_bf, P126, scb, ALU.mult, ALU.mult)
                p.stt("dve", gdq[:, base + 512:base + 1024], hi_bf, P126, scb, ALU.mult, ALU.mult)

        def gu_mms(gdq, banks, dst_fn):
            for nb in range(4):
                for t5 in range(2):
                    pa = banks[(nb * 2 + t5) % 2]
                    kk = 0
                    for qc in range(8):
                        for par in range(2):
                            p.mm(pa, gdq[:, qc * 1024 + par * 512 + nb * 128:
                                         qc * 1024 + par * 512 + nb * 128 + 128],
                                 SV(ar_xn.ap()[:, (2 * qc + par) * 1024 + t5 * 512:
                                               (2 * qc + par) * 1024 + t5 * 512 + 512], "xno"),
                                 start=(kk == 0), stop=(kk == 15))
                            kk += 1
                    dst_fn(nb, t5, pa)

        bkAB = [SV(pA.ap()[:, 0:512], "pA"), SV(pB.ap()[:, 0:512], "pB")]
        bkEF = [SV(pE.ap()[:, 0:512], "pE"), SV(pF.ap()[:, 0:512], "pF")]
        for nblk in range(32):
            nb0 = nblk * 512
            g_ = gdqs[nblk % 2]
            dequant8(tg, "l1_tg", gsc, "l1_gsc", nb0, g_)

            def st_gate(nb, t5, pa):
                p.copy("act" if t5 == 0 else "dve",
                       gblk[:, nb * 1024 + t5 * 512: nb * 1024 + t5 * 512 + 512], pa)
            gu_mms(g_, bkAB, st_gate)
            dequant8(tu, "l1_tu", usc, "l1_usc", nb0, g_)

            def st_up(nb, t5, pb):
                gc = gblk[:, nb * 1024 + t5 * 512: nb * 1024 + t5 * 512 + 512]
                p.act(g2a, gc, AF.Square)
                p.ts("dve", g2a, g2a, GELT_B, GELT_A, op0=ALU.mult, op1=ALU.add)
                p.tt("dve", g2b, g2a, gc, ALU.mult)
                p.act(g2a, g2b, AF.Tanh)
                p.ts("dve", g2a, g2a, 0.5, 0.5, op0=ALU.mult, op1=ALU.add)
                p.tt("dve", g2b, g2a, gc, ALU.mult)
                p.tt("dve", interblk[:, nb * 1024 + t5 * 512: nb * 1024 + t5 * 512 + 512],
                     g2b, pb, ALU.mult)
            gu_mms(g_, bkEF, st_up)
            dstint = inter_hbm[nb0:nb0 + 512, :].rearrange("(c p) t -> p c t", p=128)
            p.dma("ds", SV(dstint, "inter_hbm"),
                  SV(interblk.ap.rearrange("p (c t) -> p c t", t=1024), "interblk"))
        if "d_inter" in dbg_out:
            p.dma("ds", SV(dbg_out["d_inter"], "d_inter"), SV(inter_hbm, "inter_hbm"))
        p.barrier()

        if upto <= 80:
            p.materialize(block, sems)
            return nc, sorted(dbg_out)

        bi16 = ar_b.ap().bitcast(I16)
        dqL = [SV(ar_b.ap()[:, 0:512], "dqL0"), SV(ar_b.ap()[:, 1024:1536], "dqL1")]
        dqH = [SV(ar_b.ap()[:, 512:1024], "dqH0"), SV(ar_b.ap()[:, 1536:2048], "dqH1")]
        ibL = [SV(ar_b.ap()[:, 2048:3072], "ibL0"), SV(ar_b.ap()[:, 4096:5120], "ibL1")]
        ibH = [SV(ar_b.ap()[:, 3072:4096], "ibH0"), SV(ar_b.ap()[:, 5120:6144], "ibH1")]
        tdbs = [SV(bi16[:, 6144:6656], "tdb0"), SV(bi16[:, 6656:7168], "tdb1")]
        lo16d = [SV(bi16[:, 7168:7680], "lo0"), SV(bi16[:, 7680:8192], "lo1")]
        hi16d = [SV(bi16[:, 8192:8704], "hi0"), SV(bi16[:, 8704:9216], "hi1")]
        sgLd = [SV(bi16[:, 9216:9728], "sgL0"), SV(bi16[:, 9728:10240], "sgL1")]
        sgHd = [SV(bi16[:, 10240:10752], "sgH0"), SV(bi16[:, 10752:11264], "sgH1")]
        lo_bfd = [SV(ar_b.ap()[:, 7168:7680], "lo0"), SV(ar_b.ap()[:, 7680:8192], "lo1")]
        hi_bfd = [SV(ar_b.ap()[:, 8192:8704], "hi0"), SV(ar_b.ap()[:, 8704:9216], "hi1")]
        scds = [SV(sm32.ap()[:, 0:512], "scd0"), SV(sm32.ap()[:, 512:1024], "scd1")]
        stg8s = [SV(sm32.ap()[:, 1024:1536], "stg8a"), SV(sm32.ap()[:, 1536:2048], "stg8b")]
        banks8 = [SV(pA.ap()[:, 0:512], "pA"), SV(pB.ap()[:, 0:512], "pB"),
                  SV(pC.ap()[:, 0:512], "pC"), SV(pD.ap()[:, 0:512], "pD"),
                  SV(pE.ap()[:, 0:512], "pE"), SV(pF.ap()[:, 0:512], "pF"),
                  SV(pG.ap()[:, 0:512], "pG"),
                  SV(pH.ap()[:, 0:512], "pH")]
        for grp in range(4):
            dg0 = grp * 512
            for qn in range(64):
                pr = qn % 2
                tdb_, lo_, hi_, sgL_, sgH_ = tdbs[pr], lo16d[pr], hi16d[pr], sgLd[pr], sgHd[pr]
                p.dma("dg", tdb_, SV(td[qn * 128:qn * 128 + 128, dg0:dg0 + 512], "l1_td"))
                p.ts("dve", lo_, tdb_, 7, 6, ALU.bitwise_and, ALU.logical_shift_left)
                p.ts("dve", sgL_, tdb_, 8, 12, ALU.bitwise_and, ALU.logical_shift_left)
                p.tt("dve", lo_, lo_, sgL_, ALU.bitwise_or)
                p.ts("dve", hi_, tdb_, 0x70, 2, ALU.bitwise_and, ALU.logical_shift_left)
                p.ts("dve", sgH_, tdb_, 0x80, 8, ALU.bitwise_and, ALU.logical_shift_left)
                p.tt("dve", hi_, hi_, sgH_, ALU.bitwise_or)
                p.dma("ds", scds[pr], SV(dscx[qn * 128:qn * 128 + 128, dg0:dg0 + 512], "l1_dscx"))
                p.stt("dve", dqL[pr], lo_bfd[pr], P126, scds[pr], ALU.mult, ALU.mult)
                p.stt("dve", dqH[pr], hi_bfd[pr], P126, scds[pr], ALU.mult, ALU.mult)
                p.dma("ds", ibL[pr], hbm_rows(inter_hbm, (2 * qn) * 128, (2 * qn) * 128 + 128, "inter_hbm"))
                p.dma("ds", ibH[pr], hbm_rows(inter_hbm, (2 * qn + 1) * 128, (2 * qn + 1) * 128 + 128, "inter_hbm"))
                for ds_ in range(4):
                    for t5 in range(2):
                        bk = banks8[ds_ * 2 + t5]
                        p.mm(bk, dqL[pr][:, ds_ * 128:ds_ * 128 + 128], ibL[pr][:, t5 * 512:t5 * 512 + 512],
                             start=(qn == 0), stop=False)
                        p.mm(bk, dqH[pr][:, ds_ * 128:ds_ * 128 + 128], ibH[pr][:, t5 * 512:t5 * 512 + 512],
                             start=False, stop=(qn == 63))
            for ds_ in range(4):
                for t5 in range(2):
                    sx = stg8s[(ds_ * 2 + t5) % 2]
                    p.copy("dve" if (ds_ % 2 == 0) else "act", sx, banks8[ds_ * 2 + t5])
                    r0 = dg0 + ds_ * 128
                    p.dma("ds", SV(down_hbm[r0:r0 + 128, t5 * 512:t5 * 512 + 512], "down_hbm"), sx)
        if "d_down" in dbg_out:
            p.dma("ds", SV(dbg_out["d_down"], "d_down"), SV(down_hbm, "down_hbm"))
        p.barrier()

        if upto <= 81:
            p.materialize(block, sems)
            return nc, sorted(dbg_out)

        dn9 = SV(sm32.ap()[:, 0:1024], "dn9")
        g19 = SV(sm32.ap()[:, 1024:2048], "g19")
        s19 = SV(ar_c.ap()[:, 0:1024], "s19")
        x19 = SV(ar_c.ap()[:, 1024:2048], "x19")
        for dc in range(16):
            p.dma("ds", dn9, hbm_rows(down_hbm, dc * 128, dc * 128 + 128, "down_hbm"))
            p.act(g19, dn9, AF.Silu)
            p.ts("dve", idxw, c_jidx, dc * 128 * T, None, ALU.add)
            p.dma_gather(s19, hbm_rows(s1_hbm, 0, 128, "s1_hbm"), idxw)
            p.dma_gather(x19, hbm_rows(x1_hbm, 0, 128, "x1_hbm"), idxw)
            p.tt("dve", g19, s19, g19, ALU.mult)
            p.tt("dve", x19, x19, g19, ALU.add)
            p.dma("ds", SV(out[dc * 128:dc * 128 + 128, :], "out"), x19)

        import os
        if os.environ.get('KDBG'):
            global _LAST_PROG
            _LAST_PROG = p
        p.materialize(block, sems)
    return nc, sorted(dbg_out)



def _consts():
    c = {}
    c["ident_bf"] = np.eye(128, dtype=ml_dtypes.bfloat16)
    c["ident_f32"] = np.eye(128, dtype=np.float32)
    sc = np.arange(128)
    c["mask128"] = (sc[:, None] <= sc[None, :]).astype(np.float32)
    s64 = np.arange(64)
    c["mask64"] = (s64[:, None] <= s64[None, :]).astype(np.float32)
    c["ones1"] = np.ones((1, 128), np.float32)
    c["onescol"] = np.ones((128, 1), np.float32)
    oh = np.zeros((128, 1), np.float32)
    oh[127, 0] = 1.0
    c["oh127"] = oh
    c["oh16"] = np.eye(16, dtype=np.float32)
    c["pexp8"] = (np.arange(8)[:, None] == (np.arange(128)[None, :] // 16)).astype(np.float32)
    return c


def _prep_shared(i):
    f = np.float32
    A = np.ascontiguousarray
    d = {}
    d["l0_wq"] = A(np.asarray(i["L0_attn_Wq"], f)[PI_D])
    d["l0_wk"] = A(np.asarray(i["L0_attn_Wk"], f)[PI_D])
    d["l0_wv"] = A(np.asarray(i["L0_attn_Wv"], f)[PI_D])
    d["l0_ww"] = A(np.asarray(i["L0_attn_Ww"], f)[PI_D])
    d["l0_cq"] = A(np.asarray(i["L0_attn_cq"], f))
    d["l0_ck"] = A(np.asarray(i["L0_attn_ck"], f))
    d["l0_cv"] = A(np.asarray(i["L0_attn_cv"], f))
    d["l0_wf1"] = A(np.asarray(i["L0_attn_Wf1"], f)[PI_D])
    d["l0_wf2"] = A(np.asarray(i["L0_attn_Wf2"], f))
    d["l0_wb"] = A(np.asarray(i["L0_attn_Wb"], f)[PI_D])
    d["l0_wg1"] = A(np.asarray(i["L0_attn_Wg1"], f)[PI_D])
    d["l0_wg2"] = A(np.asarray(i["L0_attn_Wg2"], f))
    d["l0_bg2"] = A(np.asarray(i["L0_attn_bg2"], f).reshape(16, 128).T)
    d["l0_normw"] = A(np.asarray(i["L0_attn_norm_w"], f).reshape(128, 1))
    d["l0_wo"] = A(np.asarray(i["L0_attn_Wo"], f)[:, PI_D])
    d["l0_w0t"] = A(np.asarray(i["L0_ffn_W0"], f).T[PI_D])
    d["l0_b0"] = A(np.asarray(i["L0_ffn_bias0"], f).reshape(64, 128).T)
    d["l0_w1t"] = A(np.asarray(i["L0_ffn_W1"], f).T[:, PI_D])
    d["l0_b1"] = A(np.asarray(i["L0_ffn_bias1"], f)[PI_D].reshape(16, 128).T)
    d["l1_wqt"] = A(np.asarray(i["L1_attn_wq"], f).T[PI_D])
    d["l1_wkt"] = A(np.asarray(i["L1_attn_wk"], f).T[PI_D])
    d["l1_wvt"] = A(np.asarray(i["L1_attn_wv"], f).T[PI_D])
    d["l1_wot"] = A(np.asarray(i["L1_attn_wo"], f).T[:, PI_D])
    d["l1_fq1t"] = A(np.asarray(i["L1_attn_fq1_w"], f).T)
    d["l1_fq2t"] = A(np.asarray(i["L1_attn_fq2_w"], f).T)
    d["l1_fk1t"] = A(np.asarray(i["L1_attn_fk1_w"], f).T)
    d["l1_fk2t"] = A(np.asarray(i["L1_attn_fk2_w"], f).T)
    d["l1_fq1b"] = A(np.asarray(i["L1_attn_fq1_b"], f).reshape(1, 128))
    d["l1_fq2b"] = A(np.asarray(i["L1_attn_fq2_b"], f).reshape(1, 128))
    d["l1_fk1b"] = A(np.asarray(i["L1_attn_fk1_b"], f).reshape(1, 128))
    d["l1_fk2b"] = A(np.asarray(i["L1_attn_fk2_b"], f).reshape(1, 128))
    d["l1_normw"] = A(np.asarray(i["L1_attn_norm_w"], f).reshape(128, 1))
    d["l1_tg"] = A(np.asarray(i["L1_ffn_gate_W_packed"], np.int32)[PI_N].T)
    d["l1_tu"] = A(np.asarray(i["L1_ffn_up_W_packed"], np.int32)[PI_N].T)
    d["l1_gsc"] = A(np.asarray(i["L1_ffn_gate_scales"], f)[PI_N].T)
    d["l1_usc"] = A(np.asarray(i["L1_ffn_up_scales"], f)[PI_N].T)
    d["l1_td"] = A(np.asarray(i["L1_ffn_down_W_packed"], np.int32).T[:, PI_D])
    d["l1_dscx"] = A(np.asarray(i["L1_ffn_down_scales"], f).T[np.arange(8192) // 16][:, PI_D])
    d.update(_consts())
    return d


def make_in_maps(inputs, n_cores=8):
    shared = _prep_shared(inputs)
    x0 = np.asarray(inputs["L0_attn_hidden_states"], np.float32)
    maps = []
    for c in range(n_cores):
        b_, j = c // 2, c % 2
        m = dict(shared)
        m["x0f"] = np.ascontiguousarray(x0[b_].T[PI_D])
        m["jidx"] = (np.arange(128)[:, None] * T + j * TH).astype(np.int32)
        maps.append(m)
    return maps


def assemble(results):
    out = np.empty((B, T, D), np.float32)
    for c, r in enumerate(results):
        b_, j = c // 2, c % 2
        out[b_, j * TH:(j + 1) * TH, :] = np.asarray(r["out"], np.float32).T[:, INV_PI_D]
    return out


_NC_CACHE = {}


def kernel(**inputs):
    key = "full"
    if key not in _NC_CACHE:
        _NC_CACHE[key] = build()
    nc, _ = _NC_CACHE[key]
    in_maps = make_in_maps(inputs)
    res = run_bass_kernel_spmd(nc, in_maps, core_ids=list(range(8)))
    return assemble(res.results)
